# version 22
# baseline (speedup 1.0000x reference)
import contextlib
import numpy as np
import ml_dtypes
import concourse.bass as bass
import concourse.mybir as mybir

F32 = mybir.dt.float32
BF16 = mybir.dt.bfloat16
AF = mybir.ActivationFunctionType
ALU = mybir.AluOpType
AX = mybir.AxisListType
BF = ml_dtypes.bfloat16


class Cfg:
    def __init__(self, Bpc=2, S=2048, D=1024, H=16, F=4096, E=8, R=4):
        self.Bpc, self.S, self.D, self.H, self.F, self.E, self.R = Bpc, S, D, H, F, E, R
        self.hd = D // H
        assert self.hd == 64, "head dim must be 64"
        assert D % 128 == 0 and F % 128 == 0 and S % 512 == 0
        self.DC = D // 128
        self.FC = F // 128
        self.KC = S // 128
        self.TT = S // 128
        self.TG = 256
        self.NTG = S // self.TG
        self.NQG = S // 512
        self.HP = H // 2
        self.DH = (D + 511) // 512
        assert self.DC <= 8


class Sem:
    __slots__ = ("h", "n", "step", "name")
    def __init__(self, nc, name, step=1):
        self.h = nc.alloc_semaphore(name)
        self.n = 0
        self.step = step
        self.name = name


class MK:
    ENGINES = ("sync", "tensor", "vector", "scalar", "gpsimd")

    def __init__(self, nc):
        self.nc = nc
        self.cur = None
        self.eng_obj = None
        self.sems = {}

    def sem(self, name, step=1):
        if name not in self.sems:
            self.sems[name] = Sem(self.nc, name, step)
        return self.sems[name]

    def do(self, eng, fn, waits=(), incs=()):
        if self.cur == eng:
            for s, v in waits:
                if v > 0:
                    self.eng_obj.wait_ge(s.h, v)
            ins = fn(self.eng_obj)
            for s, k in incs:
                ins.then_inc(s.h, k * s.step)
        for s, k in incs:
            s.n += k * s.step

    def wait(self, eng, *waits):
        if self.cur == eng:
            for s, v in waits:
                if v > 0:
                    self.eng_obj.wait_ge(s.h, v)

    def run_phase(self, phase_fn, engines=None):
        engines = engines or self.ENGINES
        snap = {k: s.n for k, s in self.sems.items()}
        final = None
        with self.nc.Block() as block:
            for eng in engines:
                for k, s in self.sems.items():
                    s.n = snap.get(k, 0)
                self.cur = eng
                self.eng_obj = getattr(self.nc, eng)
                section = getattr(block, eng)

                def body(engobj, _fn=phase_fn, _snap=dict(snap)):
                    for s in self.sems.values():
                        v = _snap.get(s.name, 0)
                        if v > 0:
                            self.eng_obj.wait_ge(s.h, v)
                    _fn()
                section(body)
                final = {k: s.n for k, s in self.sems.items()}
        for k, s in self.sems.items():
            s.n = final[k]
        self.cur = None
        self.eng_obj = None


def host_constants(cfg):
    S = cfg.S
    k = np.arange(S, dtype=np.float64)
    k1 = np.floor(k / 16) * 16
    k2 = k - k1
    klimb = np.stack([k1, k2, k1, k2, k1, k2, np.ones(S)]).astype(np.float32)
    maskT = np.where(np.arange(128)[:, None] > np.arange(128)[None, :], -1e30, 0.0).astype(np.float32)
    return {
        "c_klimb": klimb.astype(BF),
        "c_maskT": maskT,
        "c_iota": np.arange(S, dtype=np.float32)[None, :],
        "c_identb": np.eye(128).astype(BF),
        "c_ident32": np.eye(128).astype(np.float32),
        "c_ones_row": np.ones((1, 128), np.float32),
        "c_ones_S": np.ones((1, 512), np.float32),
        "c_ones_Sb": np.ones((1, 512), BF),
        "c_e7": np.array([[0, 0, 0, 0, 0, 0, 1]], np.float32),
    }


def build_nc(cfg):
    nc = bass.Bass(target_bir_lowering=False, trn_type="TRN2")
    Bpc, S, D, H, F, E, R = cfg.Bpc, cfg.S, cfg.D, cfg.H, cfg.F, cfg.E, cfg.R
    DC, FC, KC, TT, TG, NTG, NQG, HP, DH = (cfg.DC, cfg.FC, cfg.KC, cfg.TT,
                                            cfg.TG, cfg.NTG, cfg.NQG, cfg.HP, cfg.DH)
    ER = E * R
    ntt = TG // 128
    qscale = float(1.0 / np.sqrt(D))

    x_in = nc.dram_tensor("x", [Bpc * S, D], F32, kind="ExternalInput")
    W_kqv = nc.dram_tensor("W_kqv", [D, 3 * D], F32, kind="ExternalInput")
    m_in = nc.dram_tensor("m", [1, H], F32, kind="ExternalInput")
    gate_WT = nc.dram_tensor("gate_WT", [D, E], F32, kind="ExternalInput")
    gate_b = nc.dram_tensor("gate_b", [1, E], F32, kind="ExternalInput")
    wi_WT = nc.dram_tensor("wi_WT", [D, F], F32, kind="ExternalInput")
    wi_bT = nc.dram_tensor("wi_bT", [128, FC], F32, kind="ExternalInput")
    wo_WT = nc.dram_tensor("wo_WT", [F, D], F32, kind="ExternalInput")
    wo_b = nc.dram_tensor("wo_b", [1, D], F32, kind="ExternalInput")
    A_allT = nc.dram_tensor("A_allT", [D, ER], F32, kind="ExternalInput")
    B_allT = nc.dram_tensor("B_allT", [ER, F], F32, kind="ExternalInput")
    out = nc.dram_tensor("out", [Bpc * S, D], F32, kind="ExternalOutput")

    consts = host_constants(cfg)
    C = {k: nc.inline_tensor(v, k) for k, v in consts.items()}

    xnT_sp = nc.dram_tensor("xnT_sp", [Bpc * DC, 128, S], BF16)
    wk_bf = nc.dram_tensor("wk_bf", [D, 3 * D], BF16)
    g_sp = nc.dram_tensor("g_sp", [Bpc * TT, 128, D], F32)
    s_sp = nc.dram_tensor("s_sp", [Bpc * TT, 128, H * 65], F32)

    mk = MK(nc)
    T = mk.sem("T"); V = mk.sem("V"); A = mk.sem("A")
    d0 = mk.sem("d0", 16)
    dXa = [mk.sem(f"dXa{j}", 16) for j in range(4)]
    dNa = [mk.sem("dNa0", 16), mk.sem("dNa1", 16)]
    dWb = [mk.sem("dWb0", 16), mk.sem("dWb1", 16)]
    dXb = [mk.sem("dXb0", 16), mk.sem("dXb1", 16)]
    dNg = [mk.sem("dNg0", 16), mk.sem("dNg1", 16)]
    dXg = [mk.sem("dXg0", 16), mk.sem("dXg1", 16)]
    dGb = [mk.sem("dGb0", 16), mk.sem("dGb1", 16)]
    dWc = [mk.sem("dWc0", 16), mk.sem("dWc1", 16)]
    dXc = [mk.sem("dXc0", 16), mk.sem("dXc1", 16)]
    dXc2 = [mk.sem("dXc20", 16), mk.sem("dXc21", 16)]
    dSc = [mk.sem("dSc0", 16), mk.sem("dSc1", 16)]
    dKl = mk.sem("dKl", 16)
    dQr = mk.sem("dQr", 16)
    dP0 = mk.sem("dP0", 16)
    dWp = [mk.sem("dWp0", 16), mk.sem("dWp1", 16)]
    dXd = [mk.sem(f"dXd{j}", 16) for j in range(4)]
    dGd = [mk.sem(f"dGd{j}", 16) for j in range(4)]
    dSd = [mk.sem(f"dSd{j}", 16) for j in range(4)]
    dOd = [mk.sem(f"dOd{j}", 16) for j in range(4)]

    sb = nc.alloc_sbuf_tensor
    s_identb = sb("s_identb", [128, 128], BF16)
    s_ident32 = sb("s_ident32", [128, 128], F32)
    s_ones_row = sb("s_ones_row", [1, 128], F32)
    s_ones_Sb = sb("s_ones_Sb", [1, 512], BF16)
    s_maskT = sb("s_maskT", [128, 128], F32)
    s_m = sb("s_m", [1, H], F32)
    s_gate_b_rep = sb("s_gate_b_rep", [128, E], F32)
    s_wo_b_rep = sb("s_wo_b_rep", [128, D], F32)
    s_wi_bT = sb("s_wi_bT", [128, FC], F32)
    s_gate_WT = sb("s_gate_WT", [128, DC * E], BF16)
    s_A_allT = sb("s_A_allT", [128, DC * ER], BF16)
    s_B_allT = sb("s_B_allT", [ER, F], BF16)
    s_stage = sb("s_stage", [1, 1152], F32)

    def phase0(t):
        p0 = t["p0"]
        loads = [
            (s_identb[:, :], C["c_identb"].ap()),
            (s_ones_Sb[:, :], C["c_ones_Sb"].ap()),
            (s_ident32[:, :], C["c_ident32"].ap()),
            (s_ones_row[:, :], C["c_ones_row"].ap()),
            (s_maskT[:, :], C["c_maskT"].ap()),
            (s_m[:, :], m_in[:, :]),
            (s_wi_bT[:, :], wi_bT[:, :]),
            (s_stage[0:1, 0:E], gate_b[:, :]),
            (s_stage[0:1, 128:128 + D], wo_b[:, :]),
        ]
        for dst, src in loads:
            mk.do("sync", lambda e, d=dst, s=src: e.dma_start(d, s), incs=[(d0, 1)])
        nl = d0.n
        mk.do("gpsimd", lambda e: e.dma_start(
                s_gate_WT[:, :].rearrange("p (c e) -> p c e", e=E),
                gate_WT[:, :].rearrange("(c p) e -> p c e", p=128)), incs=[(dP0, 1)])
        mk.do("gpsimd", lambda e: e.dma_start(
                s_A_allT[:, :].rearrange("p (c e) -> p c e", e=ER),
                A_allT[:, :].rearrange("(c p) e -> p c e", p=128)), incs=[(dP0, 1)])
        mk.do("gpsimd", lambda e: e.dma_start(s_B_allT[:, :], B_allT[:, :]), incs=[(dP0, 1)])
        mk.do("tensor", lambda e: e.matmul(p0[:, 0:E], s_ones_row[:, :], s_stage[0:1, 0:E],
                                           start=True, stop=True),
              waits=[(d0, d0.n)], incs=[(T, 1)])
        mk.do("vector", lambda e: e.tensor_copy(s_gate_b_rep[:, :], p0[:, 0:E]),
              waits=[(T, T.n)], incs=[(V, 1)])
        for i in range((D + 511) // 512):
            w = min(512, D - i * 512)
            mk.do("tensor", lambda e, i=i, w=w: e.matmul(
                    p0[:, 0:w], s_ones_row[:, :], s_stage[0:1, 128 + i * 512:128 + i * 512 + w],
                    start=True, stop=True),
                  waits=[(V, V.n)], incs=[(T, 1)])
            mk.do("vector", lambda e, i=i, w=w: e.tensor_copy(s_wo_b_rep[:, i * 512:i * 512 + w], p0[:, 0:w]),
                  waits=[(T, T.n)], incs=[(V, 1)])

    def phaseA(t):
        xr, xn, sq, st, xnT = t["xr"], t["xn"], t["sq"], t["st"], t["xnT"]
        pT = t["pT"]
        spill_counts = t["spill_counts"]
        a_done, v_done, t_done, sp_done, sqrt_done = [], [], [], [], []
        n = Bpc * TT

        def slots(i):
            sl = i % 2
            xsl = i % 4
            return (sl, xr[:, xsl * D:xsl * D + D], xn[:, sl * D:sl * D + D],
                    xnT[:, sl * D:sl * D + D], sl * 8)

        xld = []

        def xload(i):
            sl, xv, xnv, xnTv, c0 = slots(i)
            xsl = i % 4
            mk.do("sync", lambda e, i=i, xv=xv: e.dma_start(xv, x_in[i * 128:(i + 1) * 128, :]),
                  waits=[a_done[i - 4]] if i >= 4 else [], incs=[(dXa[xsl], 1)])
            xld.append(dXa[xsl].n)

        def stats1(i):
            sl, xv, xnv, xnTv, c0 = slots(i)
            xsl = i % 4
            ld = xld[i]
            mk.do("vector", lambda e, xv=xv, c0=c0: e.tensor_reduce(st[:, c0:c0 + 1], xv, AX.X, ALU.add),
                  waits=[(dXa[xsl], ld)], incs=[(V, 1)])
            mk.do("scalar", lambda e, xv=xv, c0=c0: e.activation(sq[:, :], xv, AF.Square,
                                                                 accum_out=st[:, c0 + 1:c0 + 2]),
                  waits=[(dXa[xsl], ld), (A, A.n)], incs=[(A, 1)])
            vst, ast = V.n, A.n
            mk.do("vector", lambda e, c0=c0: e.tensor_scalar(st[:, c0 + 2:c0 + 3], st[:, c0:c0 + 1],
                                                             1.0 / D, None, ALU.mult),
                  waits=[(V, vst), (A, ast)], incs=[(V, 1)])
            mk.do("vector", lambda e, c0=c0: e.tensor_scalar(st[:, c0 + 3:c0 + 4], st[:, c0 + 1:c0 + 2],
                                                             1.0 / D, None, ALU.mult), incs=[(V, 1)])
            mk.do("vector", lambda e, c0=c0: e.tensor_tensor(st[:, c0 + 4:c0 + 5], st[:, c0 + 2:c0 + 3],
                                                             st[:, c0 + 2:c0 + 3], ALU.mult),
                  waits=[(V, V.n)], incs=[(V, 1)])
            mk.do("vector", lambda e, c0=c0: e.tensor_tensor(st[:, c0 + 5:c0 + 6], st[:, c0 + 3:c0 + 4],
                                                             st[:, c0 + 4:c0 + 5], ALU.subtract),
                  waits=[(V, V.n)], incs=[(V, 1)])
            mk.do("vector", lambda e, c0=c0: e.tensor_scalar(st[:, c0 + 5:c0 + 6], st[:, c0 + 5:c0 + 6],
                                                             1e-5, None, ALU.add),
                  waits=[(V, V.n)], incs=[(V, 1)])
            mk.do("scalar", lambda e, c0=c0: e.activation(st[:, c0 + 6:c0 + 7], st[:, c0 + 5:c0 + 6],
                                                          AF.Sqrt),
                  waits=[(V, V.n)], incs=[(A, 1)])
            sqrt_done.append(A.n)

        def stats2(i):
            sl, xv, xnv, xnTv, c0 = slots(i)
            mk.do("vector", lambda e, c0=c0: e.reciprocal(st[:, c0 + 7:c0 + 8], st[:, c0 + 6:c0 + 7]),
                  waits=[(A, sqrt_done[i])], incs=[(V, 1)])
            mk.do("vector", lambda e, c0=c0: e.scalar_tensor_tensor(st[:, c0 + 4:c0 + 5], st[:, c0 + 2:c0 + 3],
                                                                    -1.0, st[:, c0 + 7:c0 + 8],
                                                                    ALU.mult, ALU.mult),
                  waits=[(V, V.n)], incs=[(V, 1)])

        def apply(i):
            sl, xv, xnv, xnTv, c0 = slots(i)
            mk.do("vector", lambda e, xv=xv, xnv=xnv, c0=c0: e.tensor_scalar(
                    xnv, xv, st[:, c0 + 7:c0 + 8], st[:, c0 + 4:c0 + 5], ALU.mult, ALU.add),
                  waits=[(V, V.n)] + ([(T, t_done[i - 2])] if i >= 2 else []), incs=[(V, 1)])
            a_done.append((V, V.n))
            for dc in range(DC):
                mk.do("tensor", lambda e, xnv=xnv, dc=dc: e.transpose(
                        pT.bitcast(BF16)[:, (dc % 8) * 128:(dc % 8) * 128 + 128],
                        xnv[:, dc * 128:(dc + 1) * 128], s_identb[:, :]),
                      waits=([a_done[-1]] + ([(V, v_done[i - 1])] if i >= 1 else [])) if dc == 0 else [],
                      incs=[(T, 1)])
            t_done.append(T.n)
            w0 = [(T, T.n)]
            if i >= 2:
                w0.append(sp_done[i - 2])
            mk.do("vector", lambda e, xnTv=xnTv: e.tensor_copy(
                    xnTv[:, 0:DC * 128], pT.bitcast(BF16)[:, 0:DC * 128]),
                  waits=w0, incs=[(V, 1)])
            v_done.append(V.n)
            bI, tI = i // TT, i % TT
            sem = dNa[sl]
            for dc in range(DC):
                mk.do("sync", lambda e, xnTv=xnTv, bI=bI, tI=tI, dc=dc: e.dma_start(
                        xnT_sp[bI * DC + dc, :, tI * 128:tI * 128 + 128],
                        xnTv[:, dc * 128:dc * 128 + 128]),
                      waits=[(V, V.n)] if dc == 0 else [], incs=[(sem, 1)])
            sp_done.append((sem, sem.n))
            spill_counts.append((dNa[0].n, dNa[1].n))

        def prologue():
            for j in range(min(4, n)):
                xload(j)
            stats1(0)
            if n > 1:
                stats1(1)

        def unit(i):
            stats2(i)
            apply(i)
            if i + 4 < n:
                xload(i + 4)
            if i + 2 < n:
                stats1(i + 2)
        return prologue, unit

    def phaseB(t):
        wi_sb, wo_sb, xnsl, interT = t["wi_sb"], t["wo_sb"], t["xnsl"], t["interT"]
        tmT, mT, lg, d4, gbuf = t["tmT"], t["mT"], t["lg"], t["d4"], t["gbuf"]
        p_t, p_base, p_g = t["p_t"], t["p_base"], t["p_g"]
        p_small = p_t
        LGC = 384
        spill_counts = t["spill_counts"]
        wv_wi = t["wv_wi"]
        wv = t["wv"]

        xn_free = {}
        pg_free = {}
        pb_free = {}
        vmask_prev = vtm_prev = wo_done = None

        def emit_B_tg(it):
            nonlocal vmask_prev, vtm_prev, wo_done
            if True:
                b, tg = divmod(it, NTG)
                sl = it % 2
                xns = xnsl[:, sl * DC * TG:(sl + 1) * DC * TG]
                t0 = b * TT + (tg * TG) // 128
                last_tile = b * TT + ((tg + 1) * TG) // 128 - 1
                sc = spill_counts[last_tile]
                for dc in range(DC):
                    w0 = []
                    if dc == 0:
                        w0 = [(dNa[0], sc[0]), (dNa[1], sc[1])]
                        if it >= 2:
                            w0.append((T, xn_free[sl]))
                    mk.do("sync", lambda e, dc=dc, b=b, tg=tg, xns=xns: e.dma_start(
                            xns[:, dc * TG:dc * TG + TG],
                            xnT_sp[b * DC + dc, :, tg * TG:tg * TG + TG]),
                          waits=w0, incs=[(dXb[sl], 1)])
                ldh = (dXb[sl], dXb[sl].n)
                for dc in range(DC):
                    w0 = []
                    if dc == 0:
                        w0 = [ldh]
                        w0 += [wv_wi] if it == 0 else [(V, vmask_prev)]
                        if it >= 1:
                            w0.append((V, vtm_prev))
                    mk.do("tensor", lambda e, dc=dc, xns=xns: e.matmul(
                            p_t[0:ER, 0:TG], s_A_allT[:, dc * ER:(dc + 1) * ER],
                            xns[:, dc * TG:dc * TG + TG],
                            start=(dc == 0), stop=(dc == DC - 1)),
                          waits=w0, incs=[(T, 1)] if dc == DC - 1 else [])
                for tt in range(ntt):
                    for dc in range(DC):
                        mk.do("tensor", lambda e, dc=dc, tt=tt, xns=xns: e.matmul(
                                p_small[:, LGC + tt * 8:LGC + tt * 8 + E],
                                xns[:, dc * TG + tt * 128:dc * TG + tt * 128 + 128],
                                s_gate_WT[:, dc * E:(dc + 1) * E],
                                start=(dc == 0), stop=(dc == DC - 1)),
                              waits=[], incs=[(T, 1)] if dc == DC - 1 else [])
                tgate = T.n
                tr_done = None
                for tt in range(ntt):
                    mk.do("vector", lambda e, tt=tt: e.tensor_tensor(
                            lg[:, tt * 16:tt * 16 + E], p_small[:, LGC + tt * 8:LGC + tt * 8 + E],
                            s_gate_b_rep[:, 0:E], ALU.add),
                          waits=[(T, tgate)], incs=[(V, 1)])
                    mk.do("vector", lambda e, tt=tt: e.tensor_reduce(
                            lg[:, tt * 16 + E:tt * 16 + E + 1], lg[:, tt * 16:tt * 16 + E], AX.X, ALU.max),
                          waits=[(V, V.n)], incs=[(V, 1)])
                    for r in range(R):
                        w0 = [(V, V.n)] if r == 0 else []
                        if r == 0 and tr_done is not None:
                            w0.append((T, tr_done))
                        mk.do("vector", lambda e, tt=tt, r=r: e.tensor_single_scalar(
                                d4.rearrange("p (e r) -> p e r", r=R)[:, :, r],
                                lg[:, tt * 16:tt * 16 + E],
                                lg[:, tt * 16 + E:tt * 16 + E + 1], ALU.subtract),
                              waits=w0, incs=[(V, 1)])
                    vd4 = V.n
                    mk.do("tensor", lambda e, tt=tt: e.transpose(
                            p_t[0:ER, 256:256 + 128], d4[:, :], s_ident32[:, :]),
                          waits=[(V, vd4)], incs=[(T, 1)])
                    tr_done = T.n
                    mk.do("vector", lambda e, tt=tt: e.tensor_single_scalar(
                            mT[:, tt * 128:tt * 128 + 128], p_t[0:ER, 256:256 + 128],
                            0.0, ALU.is_ge),
                          waits=[(T, T.n)], incs=[(V, 1)])
                vmask_prev = V.n
                w0 = [(V, V.n)]
                if wo_done is not None:
                    w0.append((T, wo_done))
                mk.do("vector", lambda e: e.tensor_tensor(tmT[:, :], p_t[0:ER, 0:TG], mT[:, :], ALU.mult),
                      waits=w0, incs=[(V, 1)])
                vtm_prev = V.n
                relu_wait = wo_done if it >= 1 else None
                pbanks = [p_base[0], p_base[1]]
                for fc in range(FC):
                    pb = pbanks[fc % 2]
                    for dc in range(DC):
                        w0 = []
                        if dc == 0:
                            if fc == 0:
                                w0.append((V, vtm_prev))
                            if fc >= 2:
                                w0.append((A, pb_free[fc % 2]))
                        mk.do("tensor", lambda e, fc=fc, dc=dc, pb=pb, xns=xns: e.matmul(
                                pb[:, 0:TG], wi_sb[:, dc * F + fc * 128:dc * F + fc * 128 + 128],
                                xns[:, dc * TG:dc * TG + TG], start=(dc == 0), stop=False),
                              waits=w0, incs=[])
                    mk.do("tensor", lambda e, fc=fc, pb=pb: e.matmul(
                            pb[:, 0:TG], s_B_allT[:, fc * 128:fc * 128 + 128], tmT[:, :],
                            start=False, stop=True), incs=[(T, 1)])
                    w0 = [(T, T.n)]
                    if fc == 0 and relu_wait is not None:
                        w0.append((T, relu_wait))
                    mk.do("scalar", lambda e, fc=fc, pb=pb: e.activation(
                            interT[:, fc * TG:(fc + 1) * TG], pb[:, 0:TG], AF.Relu,
                            bias=s_wi_bT[:, fc:fc + 1]),
                          waits=w0, incs=[(A, 1)])
                    pb_free[fc % 2] = A.n
                relu_done = A.n
                for fc in range(FC):
                    for tt in range(ntt):
                        for dh in range(DH):
                            w = min(512, D - dh * 512)
                            w0 = []
                            if fc == 0 and tt == 0 and dh == 0:
                                w0 = [(A, relu_done)]
                                if it == 0:
                                    w0.append(wv)
                                if it >= 1:
                                    w0.append((V, pg_free[it - 1]))
                            last = fc == FC - 1
                            mk.do("tensor", lambda e, fc=fc, tt=tt, dh=dh, w=w: e.matmul(
                                    p_g[tt * DH + dh][:, 0:w],
                                    interT[:, fc * TG + tt * 128:fc * TG + tt * 128 + 128],
                                    wo_sb[:, fc * D + dh * 512:fc * D + dh * 512 + w],
                                    start=(fc == 0), stop=last),
                                  waits=w0,
                                  incs=[(T, 1)] if last and tt == ntt - 1 and dh == DH - 1 else [])
                wo_done = T.n
                xn_free[sl] = T.n
                for tt in range(ntt):
                    gsl = (it * ntt + tt) % 2
                    gv = gbuf[:, gsl * D:gsl * D + D]
                    for dh in range(DH):
                        w = min(512, D - dh * 512)
                        w0 = [(T, wo_done)]
                        if dh == 0:
                            w0.append((dGb[gsl], dGb[gsl].n))
                        mk.do("vector", lambda e, tt=tt, dh=dh, w=w, gv=gv: e.tensor_tensor(
                                gv[:, dh * 512:dh * 512 + w], p_g[tt * DH + dh][:, 0:w],
                                s_wo_b_rep[:, dh * 512:dh * 512 + w], ALU.add),
                              waits=w0, incs=[(V, 1)])
                    mk.do("scalar", lambda e, gv=gv: e.activation(gv[:, 0:D], gv[:, 0:D], AF.Silu),
                          waits=[(V, V.n), (A, A.n)], incs=[(A, 1)])
                    mk.do("gpsimd", lambda e, gv=gv, t0=t0, tt=tt: e.dma_start(
                            g_sp[t0 + tt, :, :], gv[:, 0:D]),
                          waits=[(A, A.n)], incs=[(dGb[gsl], 1)])
                pg_free[it] = V.n

        return emit_B_tg

    def attn_head(tns, state, b, h, hh, vaug_v):
        ka, qa, va = tns["kaug"][hh], tns["qaug"][hh], tns["vaug"][hh]
        panel, sunT, stok, srcp = tns["panel"], tns["sunT"], tns["stok"], tns["srcp"]
        limbs, limbb, negrho, negrho_b = tns["limbs"], tns["limbb"], tns["negrho"], tns["negrho_b"]
        s_iota = tns["s_iota"]
        psc, ppv = tns["psc"], tns["ppv"]
        NS = len(psc)

        def dv(fn, waits=()):
            mk.do("vector", fn, waits=waits, incs=[(V, 1)])
        mh = lambda: s_m[0:1, h:h + 1]
        dv(lambda e: e.tensor_copy(limbb[0:1, 0:1], mh()), waits=[(V, V.n), (T, state["pmi_free_T"])])
        dv(lambda e: e.tensor_copy(limbs[0:1, 8:9], limbb[0:1, 0:1]), waits=[(V, V.n)])
        dv(lambda e: e.tensor_tensor(limbs[0:1, 9:10], mh(), limbs[0:1, 8:9], ALU.subtract),
           waits=[(V, V.n)])
        dv(lambda e: e.tensor_copy(limbb[0:1, 1:2], limbs[0:1, 9:10]), waits=[(V, V.n)])
        dv(lambda e: e.tensor_copy(limbs[0:1, 10:11], limbb[0:1, 1:2]), waits=[(V, V.n)])
        dv(lambda e: e.tensor_tensor(limbs[0:1, 11:12], limbs[0:1, 9:10], limbs[0:1, 10:11], ALU.subtract),
           waits=[(V, V.n)])
        dv(lambda e: e.tensor_copy(limbb[0:1, 2:3], limbs[0:1, 11:12]), waits=[(V, V.n)])
        dv(lambda e: e.tensor_copy(limbs[0:1, 12:13], limbb[0:1, 2:3]), waits=[(V, V.n)])
        dv(lambda e: e.tensor_copy(limbs[0:1, 0:1], limbs[0:1, 8:9]), waits=[(V, V.n)])
        dv(lambda e: e.tensor_copy(limbs[0:1, 1:2], limbs[0:1, 8:9]), waits=[(V, V.n)])
        dv(lambda e: e.tensor_copy(limbs[0:1, 2:3], limbs[0:1, 10:11]), waits=[(V, V.n)])
        dv(lambda e: e.tensor_copy(limbs[0:1, 3:4], limbs[0:1, 10:11]), waits=[(V, V.n)])
        dv(lambda e: e.tensor_copy(limbs[0:1, 4:5], limbs[0:1, 12:13]), waits=[(V, V.n)])
        dv(lambda e: e.tensor_copy(limbs[0:1, 5:6], limbs[0:1, 12:13]), waits=[(V, V.n)])
        dv(lambda e: e.tensor_copy(limbb[0:1, 0:6], limbs[0:1, 0:6]), waits=[(V, V.n)])
        dv(lambda e: e.tensor_scalar(negrho[0:1, :], s_iota[0:1, :], mh(), -1.0, ALU.mult, ALU.mult),
           waits=[(V, V.n), (dQr, dQr.n), (dKl, dKl.n)])
        dv(lambda e: e.tensor_single_scalar(negrho[0:1, :], negrho[0:1, :], 0.0, ALU.min),
           waits=[(V, V.n)])
        dv(lambda e: e.tensor_copy(negrho_b[0:1, :], negrho[0:1, :]), waits=[(V, V.n)])
        vlimb = V.n
        mk.do("gpsimd", lambda e: e.dma_start(qa[70:71, :], negrho_b[0:1, :]),
              waits=[(V, vlimb), (T, state["qrow_T"])], incs=[(dQr, 1)])
        for seg in range(S // 512):
            w0 = [(V, vlimb)] if seg == 0 else [(V, V.n)]
            if seg == 0:
                w0.append(state["scfree"][0] or (T, 0))
            mk.do("tensor", lambda e, seg=seg: e.matmul(
                    psc[0][0:6, 0:512], limbb[0:1, 0:6], s_ones_Sb[0:1, 0:512],
                    start=True, stop=True), waits=w0, incs=[(T, 1)])
            mk.do("vector", lambda e, seg=seg: e.tensor_copy(
                    qa[64:70, seg * 512:seg * 512 + 512], psc[0][0:6, 0:512]),
                  waits=[(T, T.n)], incs=[(V, 1)])
        state["scfree"][0] = (V, V.n)
        qaug_v = V.n
        pan = lambda c: panel[:, (c % 2) * S:(c % 2) * S + S]
        exp_done_c = {}

        def emit_pv(c):
            g0 = (128 * c) // 512
            for g in range(g0, NQG):
                start_abs = max(512 * g, 128 * c)
                w = 512 * (g + 1) - start_abs
                lastc = min(KC, 4 * (g + 1)) - 1
                w0 = []
                if g == g0:
                    w0.append((A, exp_done_c[c]))
                if c == 0:
                    w0.append((V, state["pvfree"][g]))
                mk.do("tensor", lambda e, c=c, g=g, start_abs=start_abs, w=w, va=va: e.matmul(
                        ppv[g][0:65, start_abs - 512 * g:start_abs - 512 * g + w],
                        va[:, c * 65:c * 65 + 65],
                        pan(c)[:, start_abs - 128 * c:start_abs - 128 * c + w],
                        start=(c == 0), stop=(c == lastc)),
                      waits=w0, incs=[(T, 1)])
                if c == lastc:
                    mk.do("vector", lambda e, g=g: e.tensor_copy(
                            sunT[0:65, 512 * g:512 * g + 512], ppv[g][0:65, 0:512]),
                          waits=[(T, T.n)], incs=[(V, 1)])
                    state["pvfree"][g] = V.n
            state["panel_free"][c % 2] = T.n

        sslot = 0
        for c in range(KC):
            segs = []
            qabs = 128 * c
            while qabs < S:
                w = min(512, S - qabs)
                segs.append((qabs, w))
                qabs += w
            for si, (q0, w) in enumerate(segs):
                slot = sslot % NS
                sslot += 1
                pk = psc[slot]
                mm_waits = [state["scfree"][slot]] if state["scfree"][slot] else []
                if c == 0 and si == 0:
                    mm_waits += [(V, qaug_v), (V, vaug_v), (dQr, dQr.n)]
                mk.do("tensor", lambda e, c=c, q0=q0, w=w, pk=pk, ka=ka, qa=qa: e.matmul(
                        pk[:, 0:w], ka[:, 128 * c:128 * c + 128], qa[:, q0:q0 + w],
                        start=True, stop=True),
                      waits=mm_waits, incs=[(T, 1)])
                tsc = T.n
                if si == 0:
                    mk.do("vector", lambda e, pk=pk: e.tensor_tensor(
                            pk[:, 0:128], pk[:, 0:128], s_maskT[:, :], ALU.add),
                          waits=[(T, tsc)], incs=[(V, 1)])
                ew = [(T, tsc)]
                if si == 0:
                    ew += [(V, V.n), (T, state["panel_free"][c % 2])]
                mk.do("scalar", lambda e, c=c, q0=q0, w=w, pk=pk: e.activation(
                        pan(c)[:, q0 - 128 * c:q0 - 128 * c + w], pk[:, 0:w], AF.Exp),
                      waits=ew, incs=[(A, 1)])
                state["scfree"][slot] = (A, A.n)
            exp_done_c[c] = A.n
            if c > 0:
                emit_pv(c - 1)
        emit_pv(KC - 1)
        pv_evac_v = V.n
        ssem = dSc[hh]
        stok_v = stok[:, hh * TT * 65:(hh + 1) * TT * 65]
        half_free = [state["scfree"][0], state["scfree"][1]]
        GPF = 7
        for grp in range((TT + GPF - 1) // GPF):
            js = list(range(grp * GPF, min(TT, grp * GPF + GPF)))
            bank = psc[grp % 2]
            for idx, j in enumerate(js):
                w0 = []
                if idx == 0:
                    w0 = [(V, pv_evac_v if grp == 0 else V.n)]
                    if half_free[grp % 2]:
                        w0.append(half_free[grp % 2])
                    if grp == 0:
                        w0.append((ssem, ssem.n))
                mk.do("tensor", lambda e, j=j, idx=idx, bank=bank: e.transpose(
                        bank[:, idx * 65:idx * 65 + 65],
                        sunT[0:65, j * 128:j * 128 + 128], s_ident32[0:65, 0:65]),
                      waits=w0, incs=[(T, 1)])
            ttr = T.n
            nj = len(js)
            mk.do("vector", lambda e, bank=bank, nj=nj, js=js, stok_v=stok_v: e.tensor_copy(
                    stok_v[:, js[0] * 65:js[0] * 65 + nj * 65], bank[:, 0:nj * 65]),
                  waits=[(T, ttr), (V, V.n)], incs=[(V, 1)])
            half_free[grp % 2] = (V, V.n)
        state["scfree"][0] = half_free[0]
        state["scfree"][1] = half_free[1]
        state["pmi_free_T"] = T.n
        vs = V.n
        for j in range(TT):
            mk.do("gpsimd", lambda e, j=j, stok_v=stok_v: e.dma_start(
                    s_sp[b * TT + j, :, 65 * h:65 * h + 65], stok_v[:, j * 65:j * 65 + 65]),
                  waits=[(V, vs)] if j == 0 else [], incs=[(ssem, 1)])

    def phaseC(t):
        wkp, xnT_full, vtb = t["wkp"], t["xnT_full"], t["vtb"]
        kaug, qaug, vaug = t["kaug"], t["qaug"], t["vaug"]
        psc, ppv = t["psc"], t["ppv"]
        NS = len(psc)
        wk_used = {0: None, 1: None}

        def load_wk(hp, first):
            sl = hp % 2
            for dc in range(DC):
                for fi in range(3):
                    w0 = []
                    if dc == 0 and fi == 0:
                        w0 = [(dP0, dP0.n)] if first else []
                        if wk_used[sl] is not None:
                            w0.append((T, wk_used[sl]))
                    mk.do("sync", lambda e, hp=hp, sl=sl, dc=dc, fi=fi: e.dma_start(
                            wkp[:, sl * DC * 384 + dc * 384 + fi * 128:sl * DC * 384 + dc * 384 + fi * 128 + 128],
                            wk_bf[dc * 128:(dc + 1) * 128, fi * D + hp * 128:fi * D + hp * 128 + 128]),
                          waits=w0, incs=[(dWp[sl], 1)])
        xr, gr, srr, orr = t["xr"], t["gr"], t["srr"], t["orr"]
        dstate = {"v_done": [], "i": 0, "ild": 0, "lds": []}

        drcp = t["drcp"]
        SW = H * 65

        def emit_combine_loads(bb, ntiles):
            sc0, sc1 = dSc[0].n, dSc[1].n
            v_done = dstate["v_done"]
            done = dstate.setdefault("ld", {})
            k0 = done.get(bb, 0)
            todo = list(range(k0, min(TT, k0 + ntiles)))
            done[bb] = k0 + len(todo)
            for tt_ in todo:
                i = bb * TT + tt_
                while dstate["i"] < dstate["ild"] - 3:
                    emit_combine_compute(1)
                sl = dstate["ild"] % 4
                dstate["ild"] += 1
                nque = dstate["ild"]
                xv, gv, sv = (xr[:, sl * D:sl * D + D], gr[:, sl * D:sl * D + D],
                              srr[:, sl * SW:sl * SW + SW])
                wfree = [(V, v_done[nque - 5])] if nque >= 5 else []
                mk.do("sync", lambda e, i=i, xv=xv: e.dma_start(xv, x_in[i * 128:(i + 1) * 128, :]),
                      waits=wfree, incs=[(dXd[sl], 1)])
                mk.do("scalar", lambda e, i=i, gv=gv: e.dma_start(gv, g_sp[i, :, :]),
                      waits=wfree + ([(dGb[0], dGb[0].n), (dGb[1], dGb[1].n)] if i == 0 else []),
                      incs=[(dGd[sl], 1)])
                mk.do("gpsimd", lambda e, i=i, sv=sv: e.dma_start(sv, s_sp[i, :, :]),
                      waits=wfree + ([(dSc[0], sc0), (dSc[1], sc1)] if tt_ == 0 else []),
                      incs=[(dSd[sl], 1)])
                dstate["lds"].append((i, [(dXd[sl], dXd[sl].n), (dGd[sl], dGd[sl].n), (dSd[sl], dSd[sl].n)]))

        def emit_combine_compute(nmax):
            v_done = dstate["v_done"]
            n = min(nmax, dstate["ild"] - dstate["i"])
            for _ in range(max(0, n)):
                i_abs = dstate["lds"][dstate["i"]][0]
                sl = dstate["i"] % 4
                dstate["i"] += 1
                xv, gv, sv, ov = (xr[:, sl * D:sl * D + D], gr[:, sl * D:sl * D + D],
                                  srr[:, sl * SW:sl * SW + SW], orr[:, sl * D:sl * D + D])
                lds = dstate["lds"][dstate["i"] - 1][1]
                mk.do("vector", lambda e, sv=sv: e.reciprocal(
                        drcp[:, 0:H], sv.rearrange("p (h e) -> p h e", e=65)[:, :, 64]),
                      waits=lds + [(V, V.n)], incs=[(V, 1)])
                for hq in range(H):
                    mk.do("vector", lambda e, hq=hq, sv=sv, gv=gv, ov=ov: e.scalar_tensor_tensor(
                            ov[:, hq * 64:hq * 64 + 64], sv[:, hq * 65:hq * 65 + 64],
                            drcp[:, hq:hq + 1], gv[:, hq * 64:hq * 64 + 64],
                            ALU.mult, ALU.mult),
                          waits=([(V, V.n), (dOd[sl], dOd[sl].n)] if hq == 0 else []),
                          incs=[(V, 1)] if hq == H - 1 else [])
                mk.do("vector", lambda e, xv=xv, ov=ov: e.tensor_tensor(ov, ov, xv, ALU.add),
                      waits=[(V, V.n)], incs=[(V, 1)])
                v_done.append(V.n)
                mk.do("gpsimd", lambda e, i_abs=i_abs, ov=ov: e.dma_start(
                        out[i_abs * 128:(i_abs + 1) * 128, :], ov),
                      waits=[(V, V.n)], incs=[(dOd[sl], 1)])

        load_wk(0, True)
        for j in range(2):
            mk.do("sync", lambda e, j=j: e.dma_start(kaug[j][64:71, :], C["c_klimb"].ap()),
                  incs=[(dKl, 1)])
        mk.do("sync", lambda e: e.dma_start(t["s_iota"][0:1, :], C["c_iota"].ap()),
              incs=[(dKl, 1)])

        state = {
            "attn_T": T.n,
            "qrow_T": T.n,
            "scfree": [None] * NS,
            "pvfree": [V.n] * NQG,
            "pmi_free_T": T.n,
            "panel_free": [T.n, T.n],
        }

        def load_xnT(b, wait_T):
            xsem = dXc[b % 2]
            xsem2 = dXc2[b % 2]
            first = {"gpsimd": True, "sync": True}
            for dc in range(DC):
                q = "gpsimd" if dc % 2 == 0 else "sync"
                w0 = []
                if first[q]:
                    first[q] = False
                    w0 = [] if b == 0 else [(T, wait_T)]
                mk.do(q, lambda e, dc=dc, b=b: e.dma_start(
                        xnT_full[:, dc * S:dc * S + S], xnT_sp[b * DC + dc, :, :]),
                      waits=w0, incs=[(xsem if q == "gpsimd" else xsem2, 1)])
            return (xsem.n, xsem2.n)

        ldxn_next = None
        for b in range(Bpc):
            xsem = dXc[b % 2]
            if b == 0:
                ldxn = load_xnT(0, None)
            else:
                ldxn = ldxn_next
            for hp in range(HP):
                if hp + 1 < HP:
                    load_wk(hp + 1, False)
                elif b + 1 < Bpc:
                    load_wk(0, False)
                wsl = hp % 2
                NTGK = S // 512
                for fi, kind in enumerate(("K", "Q", "V")):
                    fcol = fi * D + hp * 128
                    for dc in range(DC):
                        for tg in range(NTGK):
                            slot = tg % NS
                            pk = psc[slot]
                            w0 = []
                            if dc == 0 and tg == 0:
                                w0 += [(dWp[wsl], dWp[wsl].n if fi == 0 else 0),
                                       (dXc[b % 2], ldxn[0]), (dXc2[b % 2], ldxn[1]), (dKl, dKl.n)]
                                w0 = [x for x in w0 if x[1] > 0]
                            if dc == 0 and state["scfree"][slot]:
                                w0.append(state["scfree"][slot])
                                if tg == NTGK - 1 or NTGK >= NS:
                                    state["scfree"][slot] = None
                            mk.do("tensor", lambda e, fcol=fcol, fi=fi, tg=tg, dc=dc, pk=pk, wsl=wsl: e.matmul(
                                    pk[:, 0:512], wkp[:, wsl * DC * 384 + dc * 384 + fi * 128:wsl * DC * 384 + dc * 384 + fi * 128 + 128],
                                    xnT_full[:, dc * S + tg * 512:dc * S + tg * 512 + 512],
                                    start=(dc == 0), stop=(dc == DC - 1)),
                                  waits=w0, incs=[(T, 1)] if dc == DC - 1 and tg == NTGK - 1 else [])
                    tk = T.n
                    if kind == "V":
                        wk_used[wsl] = T.n
                        if hp == HP - 1 and b + 1 < Bpc:
                            ldxn_next = load_xnT(b + 1, T.n)
                    for tg in range(NTGK):
                        slot = tg % NS
                        pk = psc[slot]
                        if kind == "K":
                            mk.do("vector", lambda e, tg=tg, pk=pk: e.tensor_copy(
                                    kaug[0][0:64, tg * 512:tg * 512 + 512], pk[0:64, 0:512]),
                                  waits=([(T, tk), (T, state["attn_T"])] if tg == 0 else []), incs=[(V, 1)])
                            mk.do("vector", lambda e, tg=tg, pk=pk: e.tensor_copy(
                                    kaug[1][0:64, tg * 512:tg * 512 + 512], pk[64:128, 0:512]),
                                  incs=[(V, 1)])
                        elif kind == "Q":
                            mk.do("vector", lambda e, tg=tg, pk=pk: e.tensor_scalar(
                                    qaug[0][0:64, tg * 512:tg * 512 + 512], pk[0:64, 0:512],
                                    qscale, None, ALU.mult),
                                  waits=([(T, tk), (T, state["qrow_T"])] if tg == 0 else []), incs=[(V, 1)])
                            mk.do("vector", lambda e, tg=tg, pk=pk: e.tensor_scalar(
                                    qaug[1][0:64, tg * 512:tg * 512 + 512], pk[64:128, 0:512],
                                    qscale, None, ALU.mult),
                                  incs=[(V, 1)])
                        else:
                            mk.do("vector", lambda e, tg=tg, pk=pk: e.tensor_copy(
                                    vtb[:, tg * 512:tg * 512 + 512], pk[:, 0:512]),
                                  waits=([(T, tk), (T, state["attn_T"])] if tg == 0 else []), incs=[(V, 1)])
                        state["scfree"][slot] = (V, V.n)
                vb_free = {}
                for c in range(KC):
                    slot = c % NS
                    bank = psc[slot]
                    w0 = [(V, V.n)] if c == 0 else []
                    if c < NS and state["scfree"][slot]:
                        w0.append(state["scfree"][slot])
                    if c >= NS:
                        w0.append((V, vb_free[slot]))
                    mk.do("tensor", lambda e, c=c, bank=bank: e.transpose(
                            bank.bitcast(BF16)[:, 0:128], vtb[:, c * 128:c * 128 + 128], s_identb[:, :]),
                          waits=w0, incs=[(T, 1)])
                    tvt = T.n
                    mk.do("vector", lambda e, c=c, bank=bank: e.tensor_copy(
                            vaug[0][:, c * 65:c * 65 + 64], bank.bitcast(BF16)[:, 0:64]),
                          waits=[(T, tvt)], incs=[(V, 1)])
                    mk.do("vector", lambda e, c=c, bank=bank: e.tensor_copy(
                            vaug[1][:, c * 65:c * 65 + 64], bank.bitcast(BF16)[:, 64:128]),
                          incs=[(V, 1)])
                    vb_free[slot] = V.n
                for j in range(2):
                    mk.do("vector", lambda e, j=j: e.memset(
                            vaug[j][:, :].rearrange("p (c e) -> p c e", e=65)[:, :, 64:65], 1.0),
                          waits=[(V, V.n)], incs=[(V, 1)])
                vaug_v = V.n
                state["scfree"] = [(V, V.n)] * NS
                emit_combine_compute(max(0, dstate["ild"] - dstate["i"] - 2))
                for hh in range(2):
                    h = 2 * hp + hh
                    attn_head(t, state, b, h, hh, vaug_v)
                    state["qrow_T"] = T.n
                state["attn_T"] = T.n
                if b >= 1:
                    emit_combine_loads(b - 1, (TT + HP - 1) // HP)
            if b == Bpc - 1:
                for bb in range(Bpc):
                    while dstate.setdefault("ld", {}).get(bb, 0) < TT:
                        emit_combine_loads(bb, 2)
                        emit_combine_compute(max(0, dstate["ild"] - dstate["i"] - 2))
                emit_combine_compute(10 ** 9)
        for qd in dOd:
            mk.wait("sync", (qd, qd.n))
            mk.wait("gpsimd", (qd, qd.n))

    with contextlib.ExitStack() as st0:
        t0 = {"p0": st0.enter_context(nc.psum_tensor("p0_rep", [128, 512], F32))}
        mk.run_phase(lambda: phase0(t0), engines=("sync", "vector", "tensor", "gpsimd"))

    with contextlib.ExitStack() as stAB:
        e = stAB.enter_context
        tAB = {
            "xr": e(nc.sbuf_tensor("a_x", [128, 4 * D], F32)),
            "xn": e(nc.sbuf_tensor("a_xn", [128, 2 * D], BF16)),
            "sq": e(nc.sbuf_tensor("a_sq", [128, D], BF16)),
            "st": e(nc.sbuf_tensor("a_st", [128, 16], F32)),
            "xnT": e(nc.sbuf_tensor("a_xnT", [128, 2 * D], BF16)),
            "pT": e(nc.psum_tensor("a_pT0", [128, 512], F32)),
            "wi_sb": e(nc.sbuf_tensor("b_wi", [128, DC * F], BF16)),
            "wo_sb": e(nc.sbuf_tensor("b_wo", [128, FC * D], BF16)),
            "xnsl": e(nc.sbuf_tensor("b_xnsl", [128, 2 * DC * TG], BF16)),
            "interT": e(nc.sbuf_tensor("b_interT", [128, FC * TG], BF16)),
            "tmT": e(nc.sbuf_tensor("b_tmT", [ER, TG], BF16)),
            "mT": e(nc.sbuf_tensor("b_mT", [ER, TG], BF16)),
            "lg": e(nc.sbuf_tensor("b_lg", [128, 32], F32)),
            "d4": e(nc.sbuf_tensor("b_d4", [128, ER], F32)),
            "gbuf": e(nc.sbuf_tensor("b_g", [128, 2 * D], F32)),
            "p_t": e(nc.psum_tensor("b_pt", [128, 512], F32)),
            "p_base": [e(nc.psum_tensor("b_pbase0", [128, 512], F32)),
                       e(nc.psum_tensor("b_pbase1", [128, 512], F32))],
            "p_g": [e(nc.psum_tensor(f"b_pg{j}", [128, 512], F32)) for j in range(2 * DH)],
            "spill_counts": [],
        }
        assert 4 + 2 * DH <= 8

        def phaseAB():
            t = tAB
            def wload(dst_sb, src_dram, nrowchunks, ncols, sem):
                for rc in range(nrowchunks):
                    mk.do("gpsimd", lambda e, rc=rc, src=src_dram, dst=dst_sb, ncols=ncols: e.dma_start(
                            dst[:, rc * ncols:(rc + 1) * ncols], src[rc * 128:(rc + 1) * 128, :]),
                          incs=[(sem, 1)])
            t["spill_counts"].clear()
            wload(t["wi_sb"], wi_WT, DC, F, dWb[0])
            t["wv_wi"] = (dWb[0], dWb[0].n)
            wload(t["wo_sb"], wo_WT, FC, D, dWb[1])
            t["wv"] = (dWb[1], dWb[1].n)
            mk.do("gpsimd", lambda e: e.dma_start(wk_bf[:, :], W_kqv[:, :]), incs=[(dP0, 1)])
            proA, eA = phaseA(t)
            eB = phaseB(t)
            proA()
            NU = Bpc * NTG
            for it in range(NU):
                eA(2 * it)
                eA(2 * it + 1)
                if it >= 1:
                    eB(it - 1)
            eB(NU - 1)
        mk.run_phase(phaseAB)

    with contextlib.ExitStack() as stC:
        e = stC.enter_context
        tC = {
            "wkp": e(nc.sbuf_tensor("c_wkp", [128, 2 * DC * 384], BF16)),
            "xnT_full": e(nc.sbuf_tensor("c_xnT", [128, DC * S], BF16)),
            "kaug": [e(nc.sbuf_tensor("c_kaug0", [71, S], BF16)),
                     e(nc.sbuf_tensor("c_kaug1", [71, S], BF16))],
            "qaug": [e(nc.sbuf_tensor("c_qaug0", [71, S], BF16)),
                     e(nc.sbuf_tensor("c_qaug1", [71, S], BF16))],
            "vaug": [e(nc.sbuf_tensor("c_vaug0", [128, KC * 65], BF16)),
                     e(nc.sbuf_tensor("c_vaug1", [128, KC * 65], BF16))],
            "vtb": e(nc.sbuf_tensor("c_vtb", [128, S], BF16)),
            "panel": e(nc.sbuf_tensor("c_panel", [128, 2 * S], BF16)),
            "sunT": e(nc.sbuf_tensor("c_sunT", [65, S], F32)),
            "stok": e(nc.sbuf_tensor("c_stok", [128, 2 * TT * 65], F32)),
            "srcp": e(nc.sbuf_tensor("c_srcp", [128, 8], F32)),
            "limbs": e(nc.sbuf_tensor("c_limbs", [1, 16], F32)),
            "limbb": e(nc.sbuf_tensor("c_limbb", [1, 8], BF16)),
            "negrho": e(nc.sbuf_tensor("c_negrho", [1, S], F32)),
            "negrho_b": e(nc.sbuf_tensor("c_negrho_b", [1, S], BF16)),
            "s_iota": e(nc.sbuf_tensor("c_iota_sb", [1, S], F32)),
            "psc": [e(nc.psum_tensor(f"c_psc{j}", [128, 512], F32)) for j in range(8 - NQG)],
            "ppv": [e(nc.psum_tensor(f"c_ppv{j}", [65, 512], F32)) for j in range(NQG)],
        }
        tC["xr"] = e(nc.sbuf_tensor("d_x", [128, 4 * D], F32))
        tC["gr"] = e(nc.sbuf_tensor("d_g", [128, 4 * D], F32))
        tC["srr"] = e(nc.sbuf_tensor("d_s", [128, 4 * H * 65], F32))
        tC["drcp"] = e(nc.sbuf_tensor("d_rcp", [128, 16], F32))
        tC["orr"] = e(nc.sbuf_tensor("d_o", [128, 4 * D], F32))
        mk.run_phase(lambda: phaseC(tC))

    nc.finalize()
    return nc


def host_inputs(cfg, x, W_kqv, m, gate_W, gate_b, wi_W, wi_b, wo_W, wo_b, lora_A, lora_B):
    D, F, E, R, FC = cfg.D, cfg.F, cfg.E, cfg.R, cfg.FC
    return {
        "x": np.ascontiguousarray(x.reshape(cfg.Bpc * cfg.S, D)).astype(np.float32),
        "W_kqv": np.ascontiguousarray(W_kqv).astype(np.float32),
        "m": np.ascontiguousarray(m.reshape(1, cfg.H)).astype(np.float32),
        "gate_WT": np.ascontiguousarray(gate_W.T).astype(np.float32),
        "gate_b": np.ascontiguousarray(gate_b.reshape(1, E)).astype(np.float32),
        "wi_WT": np.ascontiguousarray(wi_W.T).astype(np.float32),
        "wi_bT": np.ascontiguousarray(wi_b.reshape(FC, 128).T).astype(np.float32),
        "wo_WT": np.ascontiguousarray(wo_W.T).astype(np.float32),
        "wo_b": np.ascontiguousarray(wo_b.reshape(1, D)).astype(np.float32),
        "A_allT": np.ascontiguousarray(lora_A.transpose(2, 0, 1).reshape(D, E * R)).astype(np.float32),
        "B_allT": np.ascontiguousarray(lora_B.transpose(0, 2, 1).reshape(E * R, F)).astype(np.float32),
    }


from concourse.bass_utils import run_bass_kernel_spmd

_N_CORES = 8

def kernel(L0_attn_x, L0_attn_W_kqv, L0_attn_m, L0_ffn_gate_W, L0_ffn_gate_b,
           L0_ffn_wi_W, L0_ffn_wi_b, L0_ffn_wo_W, L0_ffn_wo_b,
           L0_ffn_lora_A, L0_ffn_lora_B):
    x = np.asarray(L0_attn_x, dtype=np.float32)
    B, S, D = x.shape
    cfg = Cfg(Bpc=B // _N_CORES, S=S, D=D, H=16,
              F=np.asarray(L0_ffn_wi_W).shape[0],
              E=np.asarray(L0_ffn_gate_W).shape[0],
              R=np.asarray(L0_ffn_lora_A).shape[1])
    common = host_inputs(cfg, x[0:cfg.Bpc],
                         np.asarray(L0_attn_W_kqv), np.asarray(L0_attn_m),
                         np.asarray(L0_ffn_gate_W), np.asarray(L0_ffn_gate_b),
                         np.asarray(L0_ffn_wi_W), np.asarray(L0_ffn_wi_b),
                         np.asarray(L0_ffn_wo_W), np.asarray(L0_ffn_wo_b),
                         np.asarray(L0_ffn_lora_A), np.asarray(L0_ffn_lora_B))
    in_maps = []
    for c in range(_N_CORES):
        im = dict(common)
        im["x"] = np.ascontiguousarray(
            x[c * cfg.Bpc:(c + 1) * cfg.Bpc].reshape(cfg.Bpc * S, D)).astype(np.float32)
        in_maps.append(im)
    nc = build_nc(cfg)
    res = run_bass_kernel_spmd(nc, in_maps, core_ids=list(range(_N_CORES)))
    shards = [res.results[c]["out"].reshape(cfg.Bpc, S, D) for c in range(_N_CORES)]
    return np.concatenate(shards, axis=0).astype(np.float32)


# version 23
# speedup vs baseline: 1.1184x; 1.1184x over previous
import contextlib
import numpy as np
import ml_dtypes
import concourse.bass as bass
import concourse.mybir as mybir

F32 = mybir.dt.float32
BF16 = mybir.dt.bfloat16
AF = mybir.ActivationFunctionType
ALU = mybir.AluOpType
AX = mybir.AxisListType
BF = ml_dtypes.bfloat16


class Cfg:
    def __init__(self, Bpc=2, S=2048, D=1024, H=16, F=4096, E=8, R=4):
        self.Bpc, self.S, self.D, self.H, self.F, self.E, self.R = Bpc, S, D, H, F, E, R
        self.hd = D // H
        assert self.hd == 64, "head dim must be 64"
        assert D % 128 == 0 and F % 128 == 0 and S % 512 == 0
        self.DC = D // 128
        self.FC = F // 128
        self.KC = S // 128
        self.TT = S // 128
        self.TG = 256
        self.NTG = S // self.TG
        self.NQG = S // 512
        self.HP = H // 2
        self.DH = (D + 511) // 512
        assert self.DC <= 8


class Sem:
    __slots__ = ("h", "n", "step", "name")
    def __init__(self, nc, name, step=1):
        self.h = nc.alloc_semaphore(name)
        self.n = 0
        self.step = step
        self.name = name


class MK:
    ENGINES = ("sync", "tensor", "vector", "scalar", "gpsimd")

    def __init__(self, nc):
        self.nc = nc
        self.cur = None
        self.eng_obj = None
        self.sems = {}

    def sem(self, name, step=1):
        if name not in self.sems:
            self.sems[name] = Sem(self.nc, name, step)
        return self.sems[name]

    def do(self, eng, fn, waits=(), incs=()):
        if self.cur == eng:
            for s, v in waits:
                if v > 0:
                    self.eng_obj.wait_ge(s.h, v)
            ins = fn(self.eng_obj)
            for s, k in incs:
                ins.then_inc(s.h, k * s.step)
        for s, k in incs:
            s.n += k * s.step

    def wait(self, eng, *waits):
        if self.cur == eng:
            for s, v in waits:
                if v > 0:
                    self.eng_obj.wait_ge(s.h, v)

    def run_phase(self, phase_fn, engines=None):
        engines = engines or self.ENGINES
        snap = {k: s.n for k, s in self.sems.items()}
        final = None
        with self.nc.Block() as block:
            for eng in engines:
                for k, s in self.sems.items():
                    s.n = snap.get(k, 0)
                self.cur = eng
                self.eng_obj = getattr(self.nc, eng)
                section = getattr(block, eng)

                def body(engobj, _fn=phase_fn, _snap=dict(snap)):
                    for s in self.sems.values():
                        v = _snap.get(s.name, 0)
                        if v > 0:
                            self.eng_obj.wait_ge(s.h, v)
                    _fn()
                section(body)
                final = {k: s.n for k, s in self.sems.items()}
        for k, s in self.sems.items():
            s.n = final[k]
        self.cur = None
        self.eng_obj = None


def host_constants(cfg):
    S = cfg.S
    k = np.arange(S, dtype=np.float64)
    k1 = np.floor(k / 16) * 16
    k2 = k - k1
    klimb = np.stack([k1, k2, k1, k2, k1, k2, np.ones(S)]).astype(np.float32)
    maskT = np.where(np.arange(128)[:, None] > np.arange(128)[None, :], -1e30, 0.0).astype(np.float32)
    return {
        "c_klimb": klimb.astype(BF),
        "c_maskT": maskT,
        "c_iota": np.arange(S, dtype=np.float32)[None, :],
        "c_identb": np.eye(128).astype(BF),
        "c_ident32": np.eye(128).astype(np.float32),
        "c_ones_row": np.ones((1, 128), np.float32),
        "c_ones_S": np.ones((1, 512), np.float32),
        "c_ones_Sb": np.ones((1, 512), BF),
        "c_e7": np.array([[0, 0, 0, 0, 0, 0, 1]], np.float32),
        "c_sel3": np.array([[1, 1, 0, 0, 0, 0,
                             0, 0, 1, 1, 0, 0,
                             0, 0, 0, 0, 1, 1]], np.float32).astype(BF),
    }


def build_nc(cfg):
    nc = bass.Bass(target_bir_lowering=False, trn_type="TRN2")
    Bpc, S, D, H, F, E, R = cfg.Bpc, cfg.S, cfg.D, cfg.H, cfg.F, cfg.E, cfg.R
    DC, FC, KC, TT, TG, NTG, NQG, HP, DH = (cfg.DC, cfg.FC, cfg.KC, cfg.TT,
                                            cfg.TG, cfg.NTG, cfg.NQG, cfg.HP, cfg.DH)
    ER = E * R
    ntt = TG // 128
    qscale = float(1.0 / np.sqrt(D))

    x_in = nc.dram_tensor("x", [Bpc * S, D], F32, kind="ExternalInput")
    W_kqv = nc.dram_tensor("W_kqv", [D, 3 * D], F32, kind="ExternalInput")
    m_in = nc.dram_tensor("m", [1, H], F32, kind="ExternalInput")
    gate_WT = nc.dram_tensor("gate_WT", [D, E], F32, kind="ExternalInput")
    gate_b = nc.dram_tensor("gate_b", [1, E], F32, kind="ExternalInput")
    wi_WT = nc.dram_tensor("wi_WT", [D, F], F32, kind="ExternalInput")
    wi_bT = nc.dram_tensor("wi_bT", [128, FC], F32, kind="ExternalInput")
    wo_WT = nc.dram_tensor("wo_WT", [F, D], F32, kind="ExternalInput")
    wo_b = nc.dram_tensor("wo_b", [1, D], F32, kind="ExternalInput")
    A_allT = nc.dram_tensor("A_allT", [D, ER], F32, kind="ExternalInput")
    B_allT = nc.dram_tensor("B_allT", [ER, F], F32, kind="ExternalInput")
    out = nc.dram_tensor("out", [Bpc * S, D], F32, kind="ExternalOutput")

    consts = host_constants(cfg)
    C = {k: nc.inline_tensor(v, k) for k, v in consts.items()}

    xnT_sp = nc.dram_tensor("xnT_sp", [Bpc * DC, 128, S], BF16)
    wk_bf = nc.dram_tensor("wk_bf", [D, 3 * D], BF16)
    g_sp = nc.dram_tensor("g_sp", [Bpc * TT, 128, D], F32)
    s_sp = nc.dram_tensor("s_sp", [Bpc * TT, 128, H * 65], F32)

    mk = MK(nc)
    T = mk.sem("T"); V = mk.sem("V"); A = mk.sem("A")
    d0 = mk.sem("d0", 16)
    dXa = [mk.sem("dXa0", 16), mk.sem("dXa1", 16)]
    dNa = [mk.sem("dNa0", 16), mk.sem("dNa1", 16)]
    dWb = [mk.sem("dWb0", 16), mk.sem("dWb1", 16)]
    dXb = [mk.sem("dXb0", 16), mk.sem("dXb1", 16)]
    dGb = [mk.sem("dGb0", 16), mk.sem("dGb1", 16)]
    dWc = [mk.sem("dWc0", 16), mk.sem("dWc1", 16)]
    dXc = [mk.sem("dXc0", 16), mk.sem("dXc1", 16)]
    dXc2 = [mk.sem("dXc20", 16), mk.sem("dXc21", 16)]
    dSc = [mk.sem("dSc0", 16), mk.sem("dSc1", 16)]
    dKl = mk.sem("dKl", 16)
    dQr = mk.sem("dQr", 16)
    dP0 = mk.sem("dP0", 16)
    dWp = [mk.sem("dWp0", 16), mk.sem("dWp1", 16)]
    dXd = [mk.sem(f"dXd{j}", 16) for j in range(4)]
    dGd = [mk.sem(f"dGd{j}", 16) for j in range(4)]
    dSd = [mk.sem(f"dSd{j}", 16) for j in range(4)]
    dOd = [mk.sem(f"dOd{j}", 16) for j in range(4)]

    sb = nc.alloc_sbuf_tensor
    s_identb = sb("s_identb", [128, 128], BF16)
    s_ident32 = sb("s_ident32", [128, 128], F32)
    s_ones_row = sb("s_ones_row", [1, 128], F32)
    s_ones_Sb = sb("s_ones_Sb", [1, 512], BF16)
    s_maskT = sb("s_maskT", [128, 128], F32)
    s_m = sb("s_m", [1, H], F32)
    s_gate_b_rep = sb("s_gate_b_rep", [128, E], F32)
    s_wo_b_rep = sb("s_wo_b_rep", [128, D], F32)
    s_wi_bT = sb("s_wi_bT", [128, FC], F32)
    s_gate_WT = sb("s_gate_WT", [128, DC * E], BF16)
    s_A_allT = sb("s_A_allT", [128, DC * ER], BF16)
    s_B_allT = sb("s_B_allT", [ER, F], BF16)
    s_stage = sb("s_stage", [1, 1280], F32)

    def phase0(t):
        p0 = t["p0"]
        loads = [
            (s_identb[:, :], C["c_identb"].ap()),
            (s_ones_Sb[:, :], C["c_ones_Sb"].ap()),
            (s_ident32[:, :], C["c_ident32"].ap()),
            (s_ones_row[:, :], C["c_ones_row"].ap()),
            (s_maskT[:, :], C["c_maskT"].ap()),
            (s_m[:, :], m_in[:, :]),
            (s_wi_bT[:, :], wi_bT[:, :]),
            (s_stage[0:1, 0:E], gate_b[:, :]),
            (s_stage[0:1, 128:128 + D], wo_b[:, :]),
        ]
        for dst, src in loads:
            mk.do("sync", lambda e, d=dst, s=src: e.dma_start(d, s), incs=[(d0, 1)])
        nl = d0.n
        mk.do("gpsimd", lambda e: e.dma_start(
                s_gate_WT[:, :].rearrange("p (c e) -> p c e", e=E),
                gate_WT[:, :].rearrange("(c p) e -> p c e", p=128)), incs=[(dP0, 1)])
        mk.do("gpsimd", lambda e: e.dma_start(
                s_A_allT[:, :].rearrange("p (c e) -> p c e", e=ER),
                A_allT[:, :].rearrange("(c p) e -> p c e", p=128)), incs=[(dP0, 1)])
        mk.do("gpsimd", lambda e: e.dma_start(s_B_allT[:, :], B_allT[:, :]), incs=[(dP0, 1)])
        mk.do("tensor", lambda e: e.matmul(p0[:, 0:E], s_ones_row[:, :], s_stage[0:1, 0:E],
                                           start=True, stop=True),
              waits=[(d0, d0.n)], incs=[(T, 1)])
        mk.do("vector", lambda e: e.tensor_copy(s_gate_b_rep[:, :], p0[:, 0:E]),
              waits=[(T, T.n)], incs=[(V, 1)])
        for i in range((D + 511) // 512):
            w = min(512, D - i * 512)
            mk.do("tensor", lambda e, i=i, w=w: e.matmul(
                    p0[:, 0:w], s_ones_row[:, :], s_stage[0:1, 128 + i * 512:128 + i * 512 + w],
                    start=True, stop=True),
                  waits=[(V, V.n)], incs=[(T, 1)])
            mk.do("vector", lambda e, i=i, w=w: e.tensor_copy(s_wo_b_rep[:, i * 512:i * 512 + w], p0[:, 0:w]),
                  waits=[(T, T.n)], incs=[(V, 1)])

    def phaseA(t):
        xr, xn, sq, st, xnT = t["xr"], t["xn"], t["sq"], t["st"], t["xnT"]
        pT = t["pT"]
        spill_counts = t["spill_counts"]
        a_done, v_done, t_done = [], [], []
        for i in range(Bpc * TT):
            sl = i % 2
            xv = xr[:, sl * D:sl * D + D]
            xnv = xn[:, sl * D:sl * D + D]
            xnTv = xnT[:, sl * D:sl * D + D]
            c0 = sl * 8
            mk.do("sync", lambda e, i=i, xv=xv: e.dma_start(xv, x_in[i * 128:(i + 1) * 128, :]),
                  waits=[a_done[i - 2]] if i >= 2 else [], incs=[(dXa[sl], 1)])
            ld = dXa[sl].n
            mk.do("vector", lambda e, xv=xv, c0=c0: e.tensor_reduce(st[:, c0:c0 + 1], xv, AX.X, ALU.add),
                  waits=[(dXa[sl], ld)], incs=[(V, 1)])
            mk.do("scalar", lambda e, xv=xv, c0=c0: e.activation(sq[:, :], xv, AF.Square,
                                                                 accum_out=st[:, c0 + 1:c0 + 2]),
                  waits=[(dXa[sl], ld), (A, A.n)], incs=[(A, 1)])
            vst, ast = V.n, A.n
            mk.do("vector", lambda e, c0=c0: e.tensor_scalar(st[:, c0 + 2:c0 + 3], st[:, c0:c0 + 1],
                                                             1.0 / D, None, ALU.mult),
                  waits=[(V, vst), (A, ast)], incs=[(V, 1)])
            mk.do("vector", lambda e, c0=c0: e.tensor_scalar(st[:, c0 + 3:c0 + 4], st[:, c0 + 1:c0 + 2],
                                                             1.0 / D, None, ALU.mult), incs=[(V, 1)])
            mk.do("vector", lambda e, c0=c0: e.tensor_tensor(st[:, c0 + 4:c0 + 5], st[:, c0 + 2:c0 + 3],
                                                             st[:, c0 + 2:c0 + 3], ALU.mult),
                  waits=[(V, V.n)], incs=[(V, 1)])
            mk.do("vector", lambda e, c0=c0: e.tensor_tensor(st[:, c0 + 5:c0 + 6], st[:, c0 + 3:c0 + 4],
                                                             st[:, c0 + 4:c0 + 5], ALU.subtract),
                  waits=[(V, V.n)], incs=[(V, 1)])
            mk.do("vector", lambda e, c0=c0: e.tensor_scalar(st[:, c0 + 5:c0 + 6], st[:, c0 + 5:c0 + 6],
                                                             1e-5, None, ALU.add),
                  waits=[(V, V.n)], incs=[(V, 1)])
            mk.do("scalar", lambda e, c0=c0: e.activation(st[:, c0 + 6:c0 + 7], st[:, c0 + 5:c0 + 6],
                                                          AF.Sqrt),
                  waits=[(V, V.n)], incs=[(A, 1)])
            mk.do("vector", lambda e, c0=c0: e.reciprocal(st[:, c0 + 7:c0 + 8], st[:, c0 + 6:c0 + 7]),
                  waits=[(A, A.n)], incs=[(V, 1)])
            mk.do("vector", lambda e, c0=c0: e.scalar_tensor_tensor(st[:, c0 + 4:c0 + 5], st[:, c0 + 2:c0 + 3],
                                                                    -1.0, st[:, c0 + 7:c0 + 8],
                                                                    ALU.mult, ALU.mult),
                  waits=[(V, V.n)], incs=[(V, 1)])
            mk.do("vector", lambda e, xv=xv, xnv=xnv, c0=c0: e.tensor_scalar(
                    xnv, xv, st[:, c0 + 7:c0 + 8], st[:, c0 + 4:c0 + 5], ALU.mult, ALU.add),
                  waits=[(V, V.n)] + ([(T, t_done[i - 2])] if i >= 2 else []), incs=[(V, 1)])
            a_done.append((V, V.n))
            for dc in range(DC):
                mk.do("tensor", lambda e, xnv=xnv, dc=dc: e.transpose(
                        pT.bitcast(BF16)[:, (dc % 8) * 128:(dc % 8) * 128 + 128],
                        xnv[:, dc * 128:(dc + 1) * 128], s_identb[:, :]),
                      waits=([a_done[-1]] + ([(V, v_done[i - 1])] if i >= 1 else [])) if dc == 0 else [],
                      incs=[(T, 1)])
            t_done.append(T.n)
            mk.do("vector", lambda e, xnTv=xnTv: e.tensor_copy(
                    xnTv[:, 0:DC * 128], pT.bitcast(BF16)[:, 0:DC * 128]),
                  waits=[(T, T.n), (dNa[sl], dNa[sl].n)], incs=[(V, 1)])
            v_done.append(V.n)
            bI, tI = i // TT, i % TT
            for dc in range(DC):
                w0 = [(V, V.n)] if dc == 0 else []
                if dc == 0 and i >= 2:
                    w0.append((dNa[sl], spill_counts[i - 2][sl]))
                mk.do("gpsimd", lambda e, xnTv=xnTv, bI=bI, tI=tI, dc=dc: e.dma_start(
                        xnT_sp[bI * DC + dc, :, tI * 128:tI * 128 + 128],
                        xnTv[:, dc * 128:dc * 128 + 128]),
                      waits=w0, incs=[(dNa[sl], 1)])
            spill_counts.append((dNa[0].n, dNa[1].n))

    def phaseB(t):
        wi_sb, wo_sb, xnsl, interT = t["wi_sb"], t["wo_sb"], t["xnsl"], t["interT"]
        tmT, mT, lg, d4, gbuf = t["tmT"], t["mT"], t["lg"], t["d4"], t["gbuf"]
        p_t, p_base, p_g = t["p_t"], t["p_base"], t["p_g"]
        p_small = p_t
        LGC = 384
        spill_counts = t["spill_counts"]
        wv_wi = t["wv_wi"]
        wv = t["wv"]

        xn_free = {}
        pg_free = {}
        pb_free = {}
        vmask_prev = vtm_prev = wo_done = None
        for b in range(Bpc):
            for tg in range(NTG):
                it = b * NTG + tg
                sl = it % 2
                xns = xnsl[:, sl * DC * TG:(sl + 1) * DC * TG]
                t0 = b * TT + (tg * TG) // 128
                last_tile = b * TT + ((tg + 1) * TG) // 128 - 1
                sc = spill_counts[last_tile]
                for dc in range(DC):
                    w0 = []
                    if dc == 0:
                        w0 = [(dNa[0], sc[0]), (dNa[1], sc[1])]
                        if it >= 2:
                            w0.append((T, xn_free[sl]))
                    mk.do("sync", lambda e, dc=dc, b=b, tg=tg, xns=xns: e.dma_start(
                            xns[:, dc * TG:dc * TG + TG],
                            xnT_sp[b * DC + dc, :, tg * TG:tg * TG + TG]),
                          waits=w0, incs=[(dXb[sl], 1)])
                ldx = dXb[sl].n
                for dc in range(DC):
                    w0 = []
                    if dc == 0:
                        w0 = [(dXb[sl], ldx)]
                        w0 += [wv_wi] if it == 0 else [(V, vmask_prev)]
                        if it >= 1:
                            w0.append((V, vtm_prev))
                    mk.do("tensor", lambda e, dc=dc, xns=xns: e.matmul(
                            p_t[0:ER, 0:TG], s_A_allT[:, dc * ER:(dc + 1) * ER],
                            xns[:, dc * TG:dc * TG + TG],
                            start=(dc == 0), stop=(dc == DC - 1)),
                          waits=w0, incs=[(T, 1)] if dc == DC - 1 else [])
                for tt in range(ntt):
                    for dc in range(DC):
                        mk.do("tensor", lambda e, dc=dc, tt=tt, xns=xns: e.matmul(
                                p_small[:, LGC + tt * 8:LGC + tt * 8 + E],
                                xns[:, dc * TG + tt * 128:dc * TG + tt * 128 + 128],
                                s_gate_WT[:, dc * E:(dc + 1) * E],
                                start=(dc == 0), stop=(dc == DC - 1)),
                              waits=[], incs=[(T, 1)] if dc == DC - 1 else [])
                tgate = T.n
                tr_done = None
                for tt in range(ntt):
                    mk.do("vector", lambda e, tt=tt: e.tensor_tensor(
                            lg[:, tt * 16:tt * 16 + E], p_small[:, LGC + tt * 8:LGC + tt * 8 + E],
                            s_gate_b_rep[:, 0:E], ALU.add),
                          waits=[(T, tgate)], incs=[(V, 1)])
                    mk.do("vector", lambda e, tt=tt: e.tensor_reduce(
                            lg[:, tt * 16 + E:tt * 16 + E + 1], lg[:, tt * 16:tt * 16 + E], AX.X, ALU.max),
                          waits=[(V, V.n)], incs=[(V, 1)])
                    for r in range(R):
                        w0 = [(V, V.n)] if r == 0 else []
                        if r == 0 and tr_done is not None:
                            w0.append((T, tr_done))
                        mk.do("vector", lambda e, tt=tt, r=r: e.tensor_single_scalar(
                                d4.rearrange("p (e r) -> p e r", r=R)[:, :, r],
                                lg[:, tt * 16:tt * 16 + E],
                                lg[:, tt * 16 + E:tt * 16 + E + 1], ALU.subtract),
                              waits=w0, incs=[(V, 1)])
                    vd4 = V.n
                    mk.do("tensor", lambda e, tt=tt: e.transpose(
                            p_t[0:ER, 256:256 + 128], d4[:, :], s_ident32[:, :]),
                          waits=[(V, vd4)], incs=[(T, 1)])
                    tr_done = T.n
                    mk.do("vector", lambda e, tt=tt: e.tensor_single_scalar(
                            mT[:, tt * 128:tt * 128 + 128], p_t[0:ER, 256:256 + 128],
                            0.0, ALU.is_ge),
                          waits=[(T, T.n)], incs=[(V, 1)])
                vmask_prev = V.n
                w0 = [(V, V.n)]
                if wo_done is not None:
                    w0.append((T, wo_done))
                mk.do("vector", lambda e: e.tensor_tensor(tmT[:, :], p_t[0:ER, 0:TG], mT[:, :], ALU.mult),
                      waits=w0, incs=[(V, 1)])
                vtm_prev = V.n
                relu_wait = wo_done if it >= 1 else None
                pbanks = [p_base[0], p_base[1]]
                for fc in range(FC):
                    pb = pbanks[fc % 2]
                    for dc in range(DC):
                        w0 = []
                        if dc == 0:
                            if fc == 0:
                                w0.append((V, vtm_prev))
                            if fc >= 2:
                                w0.append((A, pb_free[fc % 2]))
                        mk.do("tensor", lambda e, fc=fc, dc=dc, pb=pb, xns=xns: e.matmul(
                                pb[:, 0:TG], wi_sb[:, dc * F + fc * 128:dc * F + fc * 128 + 128],
                                xns[:, dc * TG:dc * TG + TG], start=(dc == 0), stop=False),
                              waits=w0, incs=[])
                    mk.do("tensor", lambda e, fc=fc, pb=pb: e.matmul(
                            pb[:, 0:TG], s_B_allT[:, fc * 128:fc * 128 + 128], tmT[:, :],
                            start=False, stop=True), incs=[(T, 1)])
                    w0 = [(T, T.n)]
                    if fc == 0 and relu_wait is not None:
                        w0.append((T, relu_wait))
                    mk.do("scalar", lambda e, fc=fc, pb=pb: e.activation(
                            interT[:, fc * TG:(fc + 1) * TG], pb[:, 0:TG], AF.Relu,
                            bias=s_wi_bT[:, fc:fc + 1]),
                          waits=w0, incs=[(A, 1)])
                    pb_free[fc % 2] = A.n
                relu_done = A.n
                for fc in range(FC):
                    for tt in range(ntt):
                        for dh in range(DH):
                            w = min(512, D - dh * 512)
                            w0 = []
                            if fc == 0 and tt == 0 and dh == 0:
                                w0 = [(A, relu_done)]
                                if it == 0:
                                    w0.append(wv)
                                if it >= 1:
                                    w0.append((V, pg_free[it - 1]))
                            last = fc == FC - 1
                            mk.do("tensor", lambda e, fc=fc, tt=tt, dh=dh, w=w: e.matmul(
                                    p_g[tt * DH + dh][:, 0:w],
                                    interT[:, fc * TG + tt * 128:fc * TG + tt * 128 + 128],
                                    wo_sb[:, fc * D + dh * 512:fc * D + dh * 512 + w],
                                    start=(fc == 0), stop=last),
                                  waits=w0,
                                  incs=[(T, 1)] if last and tt == ntt - 1 and dh == DH - 1 else [])
                wo_done = T.n
                xn_free[sl] = T.n
                for tt in range(ntt):
                    gsl = (it * ntt + tt) % 2
                    gv = gbuf[:, gsl * D:gsl * D + D]
                    for dh in range(DH):
                        w = min(512, D - dh * 512)
                        w0 = [(T, wo_done)]
                        if dh == 0:
                            w0.append((dGb[gsl], dGb[gsl].n))
                        mk.do("vector", lambda e, tt=tt, dh=dh, w=w, gv=gv: e.tensor_tensor(
                                gv[:, dh * 512:dh * 512 + w], p_g[tt * DH + dh][:, 0:w],
                                s_wo_b_rep[:, dh * 512:dh * 512 + w], ALU.add),
                              waits=w0, incs=[(V, 1)])
                    mk.do("scalar", lambda e, gv=gv: e.activation(gv[:, 0:D], gv[:, 0:D], AF.Silu),
                          waits=[(V, V.n), (A, A.n)], incs=[(A, 1)])
                    mk.do("gpsimd", lambda e, gv=gv, t0=t0, tt=tt: e.dma_start(
                            g_sp[t0 + tt, :, :], gv[:, 0:D]),
                          waits=[(A, A.n)], incs=[(dGb[gsl], 1)])
                pg_free[it] = V.n

    def attn_head(tns, state, b, h, hh, vaug_v):
        ka, qa, va = tns["kaug"][hh], tns["qaug"][hh], tns["vaug"][hh]
        panel, sunT, stok, srcp = tns["panel"], tns["sunT"], tns["stok"], tns["srcp"]
        limbb_all, negrho_allb, ones6 = tns["limbb_all"], tns["negrho_allb"], tns["ones6"]
        psc, ppv = tns["psc"], tns["ppv"]
        NS = len(psc)

        def dv(fn, waits=()):
            mk.do("vector", fn, waits=waits, incs=[(V, 1)])
        dv(lambda e: e.tensor_scalar(qa[64:70, :], ones6[0:6, :], limbb_all[0:6, h:h + 1],
                                     None, ALU.mult),
           waits=[(V, V.n), (T, state["qrow_T"])])
        qaug_v = V.n
        mk.do("gpsimd", lambda e: e.dma_start(qa[70:71, :], negrho_allb[h:h + 1, :]),
              waits=[(V, state["aug_v"]), (T, state["qrow_T"])], incs=[(dQr, 1)])
        pan = lambda c: panel[:, (c % 2) * S:(c % 2) * S + S]
        exp_done_c = {}

        def emit_pv(c):
            g0 = (128 * c) // 512
            for g in range(g0, NQG):
                start_abs = max(512 * g, 128 * c)
                w = 512 * (g + 1) - start_abs
                lastc = min(KC, 4 * (g + 1)) - 1
                w0 = []
                if g == g0:
                    w0.append((A, exp_done_c[c]))
                if c == 0:
                    w0.append((V, state["pvfree"][g]))
                mk.do("tensor", lambda e, c=c, g=g, start_abs=start_abs, w=w, va=va: e.matmul(
                        ppv[g][0:65, start_abs - 512 * g:start_abs - 512 * g + w],
                        va[:, c * 65:c * 65 + 65],
                        pan(c)[:, start_abs - 128 * c:start_abs - 128 * c + w],
                        start=(c == 0), stop=(c == lastc)),
                      waits=w0, incs=[(T, 1)])
                if c == lastc:
                    mk.do("vector", lambda e, g=g: e.tensor_copy(
                            sunT[0:65, 512 * g:512 * g + 512], ppv[g][0:65, 0:512]),
                          waits=[(T, T.n)], incs=[(V, 1)])
                    state["pvfree"][g] = V.n
            state["panel_free"][c % 2] = T.n

        sslot = 0
        for c in range(KC):
            segs = []
            qabs = 128 * c
            while qabs < S:
                w = min(512, S - qabs)
                segs.append((qabs, w))
                qabs += w
            for si, (q0, w) in enumerate(segs):
                slot = sslot % NS
                sslot += 1
                pk = psc[slot]
                mm_waits = [state["scfree"][slot]] if state["scfree"][slot] else []
                if c == 0 and si == 0:
                    mm_waits += [(V, qaug_v), (V, vaug_v), (dQr, dQr.n)]
                mk.do("tensor", lambda e, c=c, q0=q0, w=w, pk=pk, ka=ka, qa=qa: e.matmul(
                        pk[:, 0:w], ka[:, 128 * c:128 * c + 128], qa[:, q0:q0 + w],
                        start=True, stop=True),
                      waits=mm_waits, incs=[(T, 1)])
                tsc = T.n
                if si == 0:
                    mk.do("vector", lambda e, pk=pk: e.tensor_tensor(
                            pk[:, 0:128], pk[:, 0:128], s_maskT[:, :], ALU.add),
                          waits=[(T, tsc)], incs=[(V, 1)])
                ew = [(T, tsc)]
                if si == 0:
                    ew += [(V, V.n), (T, state["panel_free"][c % 2])]
                mk.do("scalar", lambda e, c=c, q0=q0, w=w, pk=pk: e.activation(
                        pan(c)[:, q0 - 128 * c:q0 - 128 * c + w], pk[:, 0:w], AF.Exp),
                      waits=ew, incs=[(A, 1)])
                state["scfree"][slot] = (A, A.n)
            exp_done_c[c] = A.n
            if c > 0:
                emit_pv(c - 1)
        emit_pv(KC - 1)
        pv_evac_v = V.n
        ssem = dSc[hh]
        stok_v = stok[:, hh * TT * 65:(hh + 1) * TT * 65]
        half_free = [state["scfree"][0], state["scfree"][1]]
        GPF = 7
        for grp in range((TT + GPF - 1) // GPF):
            js = list(range(grp * GPF, min(TT, grp * GPF + GPF)))
            bank = psc[grp % 2]
            for idx, j in enumerate(js):
                w0 = []
                if idx == 0:
                    w0 = [(V, pv_evac_v if grp == 0 else V.n)]
                    if half_free[grp % 2]:
                        w0.append(half_free[grp % 2])
                    if grp == 0:
                        w0.append((ssem, ssem.n))
                mk.do("tensor", lambda e, j=j, idx=idx, bank=bank: e.transpose(
                        bank[:, idx * 65:idx * 65 + 65],
                        sunT[0:65, j * 128:j * 128 + 128], s_ident32[0:65, 0:65]),
                      waits=w0, incs=[(T, 1)])
            ttr = T.n
            nj = len(js)
            mk.do("vector", lambda e, bank=bank, nj=nj, js=js, stok_v=stok_v: e.tensor_copy(
                    stok_v[:, js[0] * 65:js[0] * 65 + nj * 65], bank[:, 0:nj * 65]),
                  waits=[(T, ttr), (V, V.n)], incs=[(V, 1)])
            half_free[grp % 2] = (V, V.n)
        state["scfree"][0] = half_free[0]
        state["scfree"][1] = half_free[1]
        state["pmi_free_T"] = T.n
        vs = V.n
        for j in range(TT):
            mk.do("gpsimd", lambda e, j=j, stok_v=stok_v: e.dma_start(
                    s_sp[b * TT + j, :, 65 * h:65 * h + 65], stok_v[:, j * 65:j * 65 + 65]),
                  waits=[(V, vs)] if j == 0 else [], incs=[(ssem, 1)])

    def phaseC(t):
        wkp, xnT_full, vtb = t["wkp"], t["xnT_full"], t["vtb"]
        kaug, qaug, vaug = t["kaug"], t["qaug"], t["vaug"]
        psc, ppv = t["psc"], t["ppv"]
        NS = len(psc)
        wk_used = {0: None, 1: None}

        def load_wk(hp, first):
            sl = hp % 2
            for dc in range(DC):
                for fi in range(3):
                    w0 = []
                    if dc == 0 and fi == 0:
                        w0 = [(dP0, dP0.n)] if first else []
                        if wk_used[sl] is not None:
                            w0.append((T, wk_used[sl]))
                    mk.do("sync", lambda e, hp=hp, sl=sl, dc=dc, fi=fi: e.dma_start(
                            wkp[:, sl * DC * 384 + dc * 384 + fi * 128:sl * DC * 384 + dc * 384 + fi * 128 + 128],
                            wk_bf[dc * 128:(dc + 1) * 128, fi * D + hp * 128:fi * D + hp * 128 + 128]),
                          waits=w0, incs=[(dWp[sl], 1)])
        xr, gr, srr, orr = t["xr"], t["gr"], t["srr"], t["orr"]
        dstate = {"v_done": [], "i": 0, "ild": 0, "lds": []}

        drcp = t["drcp"]
        SW = H * 65

        def emit_combine_loads(bb, ntiles):
            sc0, sc1 = dSc[0].n, dSc[1].n
            v_done = dstate["v_done"]
            done = dstate.setdefault("ld", {})
            k0 = done.get(bb, 0)
            todo = list(range(k0, min(TT, k0 + ntiles)))
            done[bb] = k0 + len(todo)
            for tt_ in todo:
                i = bb * TT + tt_
                while dstate["i"] < dstate["ild"] - 3:
                    emit_combine_compute(1)
                sl = dstate["ild"] % 4
                dstate["ild"] += 1
                nque = dstate["ild"]
                xv, gv, sv = (xr[:, sl * D:sl * D + D], gr[:, sl * D:sl * D + D],
                              srr[:, sl * SW:sl * SW + SW])
                wfree = [(V, v_done[nque - 5])] if nque >= 5 else []
                mk.do("sync", lambda e, i=i, xv=xv: e.dma_start(xv, x_in[i * 128:(i + 1) * 128, :]),
                      waits=wfree, incs=[(dXd[sl], 1)])
                mk.do("scalar", lambda e, i=i, gv=gv: e.dma_start(gv, g_sp[i, :, :]),
                      waits=wfree + ([(dGb[0], dGb[0].n), (dGb[1], dGb[1].n)] if i == 0 else []),
                      incs=[(dGd[sl], 1)])
                mk.do("gpsimd", lambda e, i=i, sv=sv: e.dma_start(sv, s_sp[i, :, :]),
                      waits=wfree + ([(dSc[0], sc0), (dSc[1], sc1)] if tt_ == 0 else []),
                      incs=[(dSd[sl], 1)])
                dstate["lds"].append((i, [(dXd[sl], dXd[sl].n), (dGd[sl], dGd[sl].n), (dSd[sl], dSd[sl].n)]))

        def emit_combine_compute(nmax):
            v_done = dstate["v_done"]
            n = min(nmax, dstate["ild"] - dstate["i"])
            for _ in range(max(0, n)):
                i_abs = dstate["lds"][dstate["i"]][0]
                sl = dstate["i"] % 4
                dstate["i"] += 1
                xv, gv, sv, ov = (xr[:, sl * D:sl * D + D], gr[:, sl * D:sl * D + D],
                                  srr[:, sl * SW:sl * SW + SW], orr[:, sl * D:sl * D + D])
                lds = dstate["lds"][dstate["i"] - 1][1]
                mk.do("vector", lambda e, sv=sv: e.reciprocal(
                        drcp[:, 0:H], sv.rearrange("p (h e) -> p h e", e=65)[:, :, 64]),
                      waits=lds + [(V, V.n)], incs=[(V, 1)])
                for hq in range(H):
                    mk.do("vector", lambda e, hq=hq, sv=sv, gv=gv, ov=ov: e.scalar_tensor_tensor(
                            ov[:, hq * 64:hq * 64 + 64], sv[:, hq * 65:hq * 65 + 64],
                            drcp[:, hq:hq + 1], gv[:, hq * 64:hq * 64 + 64],
                            ALU.mult, ALU.mult),
                          waits=([(V, V.n), (dOd[sl], dOd[sl].n)] if hq == 0 else []),
                          incs=[(V, 1)] if hq == H - 1 else [])
                mk.do("vector", lambda e, xv=xv, ov=ov: e.tensor_tensor(ov, ov, xv, ALU.add),
                      waits=[(V, V.n)], incs=[(V, 1)])
                v_done.append(V.n)
                mk.do("gpsimd", lambda e, i_abs=i_abs, ov=ov: e.dma_start(
                        out[i_abs * 128:(i_abs + 1) * 128, :], ov),
                      waits=[(V, V.n)], incs=[(dOd[sl], 1)])

        load_wk(0, True)
        for j in range(2):
            mk.do("sync", lambda e, j=j: e.dma_start(kaug[j][64:71, :], C["c_klimb"].ap()),
                  incs=[(dKl, 1)])
        mk.do("sync", lambda e: e.dma_start(t["s_iota"][0:1, :], C["c_iota"].ap()),
              incs=[(dKl, 1)])
        mk.do("sync", lambda e: e.dma_start(t["sel3"][:, :], C["c_sel3"].ap()),
              incs=[(dKl, 1)])

        limbs, limbb = t["limbs"], t["limbb"]
        limbb_all, negrho_allb, ones6 = t["limbb_all"], t["negrho_allb"], t["ones6"]

        def dvs(fn, waits=()):
            mk.do("vector", fn, waits=waits, incs=[(V, 1)])
        dvs(lambda e: e.tensor_copy(limbb[0:1, 0:H], s_m[0:1, 0:H]))
        dvs(lambda e: e.tensor_copy(limbs[0:1, 0:H], limbb[0:1, 0:H]), waits=[(V, V.n)])
        dvs(lambda e: e.tensor_tensor(limbs[0:1, H:2 * H], s_m[0:1, 0:H], limbs[0:1, 0:H],
                                      ALU.subtract), waits=[(V, V.n)])
        dvs(lambda e: e.tensor_copy(limbb[0:1, H:2 * H], limbs[0:1, H:2 * H]), waits=[(V, V.n)])
        dvs(lambda e: e.tensor_copy(limbs[0:1, 2 * H:3 * H], limbb[0:1, H:2 * H]), waits=[(V, V.n)])
        dvs(lambda e: e.tensor_tensor(limbs[0:1, 3 * H:4 * H], limbs[0:1, H:2 * H],
                                      limbs[0:1, 2 * H:3 * H], ALU.subtract), waits=[(V, V.n)])
        dvs(lambda e: e.tensor_copy(limbb[0:1, 2 * H:3 * H], limbs[0:1, 3 * H:4 * H]),
            waits=[(V, V.n)])
        vrows = V.n
        for k in range(3):
            mk.do("tensor", lambda e, k=k: e.matmul(
                    ppv[0][0:6, 0:H], t["sel3"][0:1, k * 6:k * 6 + 6],
                    limbb[0:1, k * H:(k + 1) * H], start=(k == 0), stop=(k == 2)),
                  waits=[(V, vrows), (dKl, dKl.n)] if k == 0 else [],
                  incs=[(T, 1)] if k == 2 else [])
        dvs(lambda e: e.tensor_copy(limbb_all[0:6, 0:H], ppv[0][0:6, 0:H]), waits=[(T, T.n)])
        dvs(lambda e: e.memset(ones6[0:6, :], 1.0), waits=[(V, V.n)])
        for seg in range(S // 512):
            pn = ppv[(seg + 1) % NQG]
            mk.do("tensor", lambda e, seg=seg, pn=pn: e.matmul(
                    pn[0:H, 0:512], s_m[0:1, 0:H],
                    t["s_iota"][0:1, seg * 512:seg * 512 + 512], start=True, stop=True),
                  waits=[(V, V.n)] + ([(dKl, dKl.n)] if seg == 0 else []), incs=[(T, 1)])
            dvs(lambda e, seg=seg, pn=pn: e.tensor_scalar(
                    negrho_allb[0:H, seg * 512:seg * 512 + 512], pn[0:H, 0:512],
                    -1.0, 0.0, ALU.mult, ALU.min), waits=[(T, T.n)])

        state = {
            "attn_T": T.n,
            "qrow_T": T.n,
            "scfree": [None] * NS,
            "pvfree": [V.n] * NQG,
            "pmi_free_T": T.n,
            "panel_free": [T.n, T.n],
            "aug_v": V.n,
        }

        def load_xnT(b, wait_T):
            xsem = dXc[b % 2]
            xsem2 = dXc2[b % 2]
            first = {"gpsimd": True, "sync": True}
            for dc in range(DC):
                q = "gpsimd" if dc % 2 == 0 else "sync"
                w0 = []
                if first[q]:
                    first[q] = False
                    w0 = ([(dNa[0], dNa[0].n), (dNa[1], dNa[1].n)] if b == 0
                          else [(T, wait_T)])
                mk.do(q, lambda e, dc=dc, b=b: e.dma_start(
                        xnT_full[:, dc * S:dc * S + S], xnT_sp[b * DC + dc, :, :]),
                      waits=w0, incs=[(xsem if q == "gpsimd" else xsem2, 1)])
            return (xsem.n, xsem2.n)

        ldxn_next = None
        for b in range(Bpc):
            xsem = dXc[b % 2]
            if b == 0:
                ldxn = load_xnT(0, None)
            else:
                ldxn = ldxn_next
            for hp in range(HP):
                if hp + 1 < HP:
                    load_wk(hp + 1, False)
                elif b + 1 < Bpc:
                    load_wk(0, False)
                wsl = hp % 2
                NTGK = S // 512
                for fi, kind in enumerate(("K", "Q", "V")):
                    fcol = fi * D + hp * 128
                    for dc in range(DC):
                        for tg in range(NTGK):
                            slot = tg % NS
                            pk = psc[slot]
                            w0 = []
                            if dc == 0 and tg == 0:
                                w0 += [(dWp[wsl], dWp[wsl].n if fi == 0 else 0),
                                       (dXc[b % 2], ldxn[0]), (dXc2[b % 2], ldxn[1]), (dKl, dKl.n)]
                                w0 = [x for x in w0 if x[1] > 0]
                            if dc == 0 and state["scfree"][slot]:
                                w0.append(state["scfree"][slot])
                                if tg == NTGK - 1 or NTGK >= NS:
                                    state["scfree"][slot] = None
                            mk.do("tensor", lambda e, fcol=fcol, fi=fi, tg=tg, dc=dc, pk=pk, wsl=wsl: e.matmul(
                                    pk[:, 0:512], wkp[:, wsl * DC * 384 + dc * 384 + fi * 128:wsl * DC * 384 + dc * 384 + fi * 128 + 128],
                                    xnT_full[:, dc * S + tg * 512:dc * S + tg * 512 + 512],
                                    start=(dc == 0), stop=(dc == DC - 1)),
                                  waits=w0, incs=[(T, 1)] if dc == DC - 1 and tg == NTGK - 1 else [])
                    tk = T.n
                    if kind == "V":
                        wk_used[wsl] = T.n
                        if hp == HP - 1 and b + 1 < Bpc:
                            ldxn_next = load_xnT(b + 1, T.n)
                    for tg in range(NTGK):
                        slot = tg % NS
                        pk = psc[slot]
                        if kind == "K":
                            mk.do("vector", lambda e, tg=tg, pk=pk: e.tensor_copy(
                                    kaug[0][0:64, tg * 512:tg * 512 + 512], pk[0:64, 0:512]),
                                  waits=([(T, tk), (T, state["attn_T"])] if tg == 0 else []), incs=[(V, 1)])
                            mk.do("vector", lambda e, tg=tg, pk=pk: e.tensor_copy(
                                    kaug[1][0:64, tg * 512:tg * 512 + 512], pk[64:128, 0:512]),
                                  incs=[(V, 1)])
                        elif kind == "Q":
                            mk.do("vector", lambda e, tg=tg, pk=pk: e.tensor_scalar(
                                    qaug[0][0:64, tg * 512:tg * 512 + 512], pk[0:64, 0:512],
                                    qscale, None, ALU.mult),
                                  waits=([(T, tk), (T, state["qrow_T"])] if tg == 0 else []), incs=[(V, 1)])
                            mk.do("vector", lambda e, tg=tg, pk=pk: e.tensor_scalar(
                                    qaug[1][0:64, tg * 512:tg * 512 + 512], pk[64:128, 0:512],
                                    qscale, None, ALU.mult),
                                  incs=[(V, 1)])
                        else:
                            mk.do("vector", lambda e, tg=tg, pk=pk: e.tensor_copy(
                                    vtb[:, tg * 512:tg * 512 + 512], pk[:, 0:512]),
                                  waits=([(T, tk), (T, state["attn_T"])] if tg == 0 else []), incs=[(V, 1)])
                        state["scfree"][slot] = (V, V.n)
                vb_free = {}
                for c in range(KC):
                    slot = c % NS
                    bank = psc[slot]
                    w0 = [(V, V.n)] if c == 0 else []
                    if c < NS and state["scfree"][slot]:
                        w0.append(state["scfree"][slot])
                    if c >= NS:
                        w0.append((V, vb_free[slot]))
                    mk.do("tensor", lambda e, c=c, bank=bank: e.transpose(
                            bank.bitcast(BF16)[:, 0:128], vtb[:, c * 128:c * 128 + 128], s_identb[:, :]),
                          waits=w0, incs=[(T, 1)])
                    tvt = T.n
                    mk.do("vector", lambda e, c=c, bank=bank: e.tensor_copy(
                            vaug[0][:, c * 65:c * 65 + 64], bank.bitcast(BF16)[:, 0:64]),
                          waits=[(T, tvt)], incs=[(V, 1)])
                    mk.do("vector", lambda e, c=c, bank=bank: e.tensor_copy(
                            vaug[1][:, c * 65:c * 65 + 64], bank.bitcast(BF16)[:, 64:128]),
                          incs=[(V, 1)])
                    vb_free[slot] = V.n
                for j in range(2):
                    mk.do("vector", lambda e, j=j: e.memset(
                            vaug[j][:, :].rearrange("p (c e) -> p c e", e=65)[:, :, 64:65], 1.0),
                          waits=[(V, V.n)], incs=[(V, 1)])
                vaug_v = V.n
                state["scfree"] = [(V, V.n)] * NS
                emit_combine_compute(max(0, dstate["ild"] - dstate["i"] - 2))
                for hh in range(2):
                    h = 2 * hp + hh
                    attn_head(t, state, b, h, hh, vaug_v)
                    state["qrow_T"] = T.n
                state["attn_T"] = T.n
                if b >= 1:
                    emit_combine_loads(b - 1, (TT + HP - 1) // HP)
            if b == Bpc - 1:
                for bb in range(Bpc):
                    while dstate.setdefault("ld", {}).get(bb, 0) < TT:
                        emit_combine_loads(bb, 2)
                        emit_combine_compute(max(0, dstate["ild"] - dstate["i"] - 2))
                emit_combine_compute(10 ** 9)
        for qd in dOd:
            mk.wait("sync", (qd, qd.n))
            mk.wait("gpsimd", (qd, qd.n))

    with contextlib.ExitStack() as st0:
        t0 = {"p0": st0.enter_context(nc.psum_tensor("p0_rep", [128, 512], F32))}
        mk.run_phase(lambda: phase0(t0), engines=("sync", "vector", "tensor", "gpsimd"))

    with contextlib.ExitStack() as stAB:
        e = stAB.enter_context
        tAB = {
            "xr": e(nc.sbuf_tensor("a_x", [128, 2 * D], F32)),
            "xn": e(nc.sbuf_tensor("a_xn", [128, 2 * D], BF16)),
            "sq": e(nc.sbuf_tensor("a_sq", [128, D], BF16)),
            "st": e(nc.sbuf_tensor("a_st", [128, 16], F32)),
            "xnT": e(nc.sbuf_tensor("a_xnT", [128, 2 * D], BF16)),
            "pT": e(nc.psum_tensor("a_pT0", [128, 512], F32)),
            "wi_sb": e(nc.sbuf_tensor("b_wi", [128, DC * F], BF16)),
            "wo_sb": e(nc.sbuf_tensor("b_wo", [128, FC * D], BF16)),
            "xnsl": e(nc.sbuf_tensor("b_xnsl", [128, 2 * DC * TG], BF16)),
            "interT": e(nc.sbuf_tensor("b_interT", [128, FC * TG], BF16)),
            "tmT": e(nc.sbuf_tensor("b_tmT", [ER, TG], BF16)),
            "mT": e(nc.sbuf_tensor("b_mT", [ER, TG], BF16)),
            "lg": e(nc.sbuf_tensor("b_lg", [128, 32], F32)),
            "d4": e(nc.sbuf_tensor("b_d4", [128, ER], F32)),
            "gbuf": e(nc.sbuf_tensor("b_g", [128, 2 * D], F32)),
            "p_t": e(nc.psum_tensor("b_pt", [128, 512], F32)),
            "p_base": [e(nc.psum_tensor("b_pbase0", [128, 512], F32)),
                       e(nc.psum_tensor("b_pbase1", [128, 512], F32))],
            "p_g": [e(nc.psum_tensor(f"b_pg{j}", [128, 512], F32)) for j in range(2 * DH)],
            "spill_counts": [],
        }
        assert 4 + 2 * DH <= 8

        def phaseAB():
            t = tAB
            def wload(dst_sb, src_dram, nrowchunks, ncols, sem):
                for rc in range(nrowchunks):
                    mk.do("gpsimd", lambda e, rc=rc, src=src_dram, dst=dst_sb, ncols=ncols: e.dma_start(
                            dst[:, rc * ncols:(rc + 1) * ncols], src[rc * 128:(rc + 1) * 128, :]),
                          incs=[(sem, 1)])
            t["spill_counts"].clear()
            wload(t["wi_sb"], wi_WT, DC, F, dWb[0])
            t["wv_wi"] = (dWb[0], dWb[0].n)
            phaseA(t)
            wload(t["wo_sb"], wo_WT, FC, D, dWb[1])
            t["wv"] = (dWb[1], dWb[1].n)
            mk.do("gpsimd", lambda e: e.dma_start(wk_bf[:, :], W_kqv[:, :]), incs=[(dP0, 1)])
            phaseB(t)
        mk.run_phase(phaseAB)

    with contextlib.ExitStack() as stC:
        e = stC.enter_context
        tC = {
            "wkp": e(nc.sbuf_tensor("c_wkp", [128, 2 * DC * 384], BF16)),
            "xnT_full": e(nc.sbuf_tensor("c_xnT", [128, DC * S], BF16)),
            "kaug": [e(nc.sbuf_tensor("c_kaug0", [71, S], BF16)),
                     e(nc.sbuf_tensor("c_kaug1", [71, S], BF16))],
            "qaug": [e(nc.sbuf_tensor("c_qaug0", [71, S], BF16)),
                     e(nc.sbuf_tensor("c_qaug1", [71, S], BF16))],
            "vaug": [e(nc.sbuf_tensor("c_vaug0", [128, KC * 65], BF16)),
                     e(nc.sbuf_tensor("c_vaug1", [128, KC * 65], BF16))],
            "vtb": e(nc.sbuf_tensor("c_vtb", [128, S], BF16)),
            "panel": e(nc.sbuf_tensor("c_panel", [128, 2 * S], BF16)),
            "sunT": e(nc.sbuf_tensor("c_sunT", [65, S], F32)),
            "stok": e(nc.sbuf_tensor("c_stok", [128, 2 * TT * 65], F32)),
            "srcp": e(nc.sbuf_tensor("c_srcp", [128, 8], F32)),
            "limbs": e(nc.sbuf_tensor("c_limbs", [1, 4 * H], F32)),
            "limbb": e(nc.sbuf_tensor("c_limbb", [1, 3 * H], BF16)),
            "limbb_all": e(nc.sbuf_tensor("c_limbb_all", [6, H], F32)),
            "sel3": e(nc.sbuf_tensor("c_sel3_sb", [1, 18], BF16)),
            "negrho_allb": e(nc.sbuf_tensor("c_negrho_allb", [H, S], BF16)),
            "ones6": e(nc.sbuf_tensor("c_ones6", [6, S], BF16)),
            "s_iota": e(nc.sbuf_tensor("c_iota_sb", [1, S], F32)),
            "psc": [e(nc.psum_tensor(f"c_psc{j}", [128, 512], F32)) for j in range(8 - NQG)],
            "ppv": [e(nc.psum_tensor(f"c_ppv{j}", [65, 512], F32)) for j in range(NQG)],
        }
        tC["xr"] = e(nc.sbuf_tensor("d_x", [128, 4 * D], F32))
        tC["gr"] = e(nc.sbuf_tensor("d_g", [128, 4 * D], F32))
        tC["srr"] = e(nc.sbuf_tensor("d_s", [128, 4 * H * 65], F32))
        tC["drcp"] = e(nc.sbuf_tensor("d_rcp", [128, 16], F32))
        tC["orr"] = e(nc.sbuf_tensor("d_o", [128, 4 * D], F32))
        mk.run_phase(lambda: phaseC(tC))

    nc.finalize()
    return nc


def host_inputs(cfg, x, W_kqv, m, gate_W, gate_b, wi_W, wi_b, wo_W, wo_b, lora_A, lora_B):
    D, F, E, R, FC = cfg.D, cfg.F, cfg.E, cfg.R, cfg.FC
    return {
        "x": np.ascontiguousarray(x.reshape(cfg.Bpc * cfg.S, D)).astype(np.float32),
        "W_kqv": np.ascontiguousarray(W_kqv).astype(np.float32),
        "m": np.ascontiguousarray(m.reshape(1, cfg.H)).astype(np.float32),
        "gate_WT": np.ascontiguousarray(gate_W.T).astype(np.float32),
        "gate_b": np.ascontiguousarray(gate_b.reshape(1, E)).astype(np.float32),
        "wi_WT": np.ascontiguousarray(wi_W.T).astype(np.float32),
        "wi_bT": np.ascontiguousarray(wi_b.reshape(FC, 128).T).astype(np.float32),
        "wo_WT": np.ascontiguousarray(wo_W.T).astype(np.float32),
        "wo_b": np.ascontiguousarray(wo_b.reshape(1, D)).astype(np.float32),
        "A_allT": np.ascontiguousarray(lora_A.transpose(2, 0, 1).reshape(D, E * R)).astype(np.float32),
        "B_allT": np.ascontiguousarray(lora_B.transpose(0, 2, 1).reshape(E * R, F)).astype(np.float32),
    }


from concourse.bass_utils import run_bass_kernel_spmd

_N_CORES = 8

def kernel(L0_attn_x, L0_attn_W_kqv, L0_attn_m, L0_ffn_gate_W, L0_ffn_gate_b,
           L0_ffn_wi_W, L0_ffn_wi_b, L0_ffn_wo_W, L0_ffn_wo_b,
           L0_ffn_lora_A, L0_ffn_lora_B):
    x = np.asarray(L0_attn_x, dtype=np.float32)
    B, S, D = x.shape
    cfg = Cfg(Bpc=B // _N_CORES, S=S, D=D, H=16,
              F=np.asarray(L0_ffn_wi_W).shape[0],
              E=np.asarray(L0_ffn_gate_W).shape[0],
              R=np.asarray(L0_ffn_lora_A).shape[1])
    common = host_inputs(cfg, x[0:cfg.Bpc],
                         np.asarray(L0_attn_W_kqv), np.asarray(L0_attn_m),
                         np.asarray(L0_ffn_gate_W), np.asarray(L0_ffn_gate_b),
                         np.asarray(L0_ffn_wi_W), np.asarray(L0_ffn_wi_b),
                         np.asarray(L0_ffn_wo_W), np.asarray(L0_ffn_wo_b),
                         np.asarray(L0_ffn_lora_A), np.asarray(L0_ffn_lora_B))
    in_maps = []
    for c in range(_N_CORES):
        im = dict(common)
        im["x"] = np.ascontiguousarray(
            x[c * cfg.Bpc:(c + 1) * cfg.Bpc].reshape(cfg.Bpc * S, D)).astype(np.float32)
        in_maps.append(im)
    nc = build_nc(cfg)
    res = run_bass_kernel_spmd(nc, in_maps, core_ids=list(range(_N_CORES)))
    shards = [res.results[c]["out"].reshape(cfg.Bpc, S, D) for c in range(_N_CORES)]
    return np.concatenate(shards, axis=0).astype(np.float32)
